# version 16
# baseline (speedup 1.0000x reference)
import numpy as np
import ml_dtypes
from contextlib import ExitStack

import concourse.bass as bass
import concourse.mybir as mybir
from concourse.bass_utils import run_bass_kernel_spmd
from concourse.bass_core_rust import IndirectOffsetOnAxis

F32 = mybir.dt.float32
BF16 = mybir.dt.bfloat16
I32 = mybir.dt.int32
AF = mybir.ActivationFunctionType
ALU = mybir.AluOpType
AX = mybir.AxisListType

B_CORE = 2
T = 2048
D = 1024
H = 16
E = 8
NT = 16
EPS = 1e-5
VS = 80
CAP = 384
NG = CAP // 128
NSLOT = 48


class Sem:
    def __init__(self, name, step):
        self.name, self.step, self.v, self.h = name, step, 0, None


class Buf:
    __slots__ = ("name", "w", "r")

    def __init__(self, name):
        self.name, self.w, self.r = name, {}, {}

    def merge_from(self, *others):
        for o in others:
            for s, c in o.w.items():
                self.w[s] = max(self.w.get(s, 0), c)
            for s, c in o.r.items():
                self.r[s] = max(self.r.get(s, 0), c)


class Prog:
    def __init__(self):
        self.plan = {e: [] for e in ("pe", "act", "dve", "pool", "sp")}
        self.sems = {nm: Sem(nm, step) for nm, step in
                     (("spe", 1), ("sact", 1), ("sdve", 1), ("spool", 1),
                      ("sx0", 16), ("sx1", 16), ("so", 16), ("swc", 16),
                      ("swh0", 16), ("swh1", 16), ("sw10", 16), ("sw11", 16),
                      ("sw20", 16), ("sw21", 16), ("sp2t", 16), ("sln0", 16),
                      ("sln1", 16), ("sdel", 16), ("sidx", 16), ("spm", 16),
                      ("sg0", 16), ("sg1", 16), ("sg2", 16), ("sg3", 16),
                      ("sds", 16), ("sdl0", 16), ("sdl1", 16),
                      ("ssd0", 16), ("ssd1", 16), ("ssd2", 16), ("ssd3", 16))}
        self.engsem = {"pe": self.sems["spe"], "act": self.sems["sact"],
                       "dve": self.sems["sdve"], "pool": self.sems["spool"]}
        self.waited = {e: {} for e in self.plan}

    def wait(self, eng, sem, val):
        if val <= 0:
            return
        hw = self.waited[eng]
        if hw.get(sem, 0) >= val:
            return
        hw[sem] = val
        self.plan[eng].append(("w", sem, val))

    def emit(self, eng, fn, reads=(), writes=(), sem=None):
        if sem is None:
            sem = self.engsem[eng]
        need = {}
        for b in tuple(reads) + tuple(writes):
            for s, c in b.w.items():
                if s is not sem:
                    need[s] = max(need.get(s, 0), c)
        for b in writes:
            for s, c in b.r.items():
                if s is not sem:
                    need[s] = max(need.get(s, 0), c)
        for s, c in need.items():
            self.wait(eng, s, c)
        sem.v += sem.step
        self.plan[eng].append(("i", fn, sem, sem.step))
        for b in reads:
            b.r[sem] = sem.v
        for b in writes:
            b.w[sem] = sem.v
            b.r.pop(sem, None)
        return sem.v

    def check(self):
        for eng, hw in self.waited.items():
            for sem, val in hw.items():
                assert val <= sem.v, (eng, sem.name, val, sem.v)

    def run(self, eng, EG):
        for item in self.plan[eng]:
            if item[0] == "w":
                EG.wait_ge(item[1].h, item[2])
            else:
                item[1](EG).then_inc(item[2].h, item[3])


def build_nc(debug=False):
    nc = bass.Bass("TRN2", target_bir_lowering=False)

    x_d = nc.dram_tensor("x", [B_CORE, T, D], F32, kind="ExternalInput").ap()
    wq_d = nc.dram_tensor("wq", [D, D], F32, kind="ExternalInput").ap()
    wk_d = nc.dram_tensor("wk", [D, D], F32, kind="ExternalInput").ap()
    wv_d = nc.dram_tensor("wv", [D, D], F32, kind="ExternalInput").ap()
    wo_d = nc.dram_tensor("wo", [D, D], F32, kind="ExternalInput").ap()
    w1_d = nc.dram_tensor("w1", [E, 16, 128, 2048], F32, kind="ExternalInput").ap()
    w2_d = nc.dram_tensor("w2", [E, 8, 128, 4096], F32, kind="ExternalInput").ap()
    b1_d = nc.dram_tensor("b1", [128, E * 32], F32, kind="ExternalInput").ap()
    b2_d = nc.dram_tensor("b2", [128, E * 8], F32, kind="ExternalInput").ap()
    gam_d = nc.dram_tensor("gam", [128, E * 8], F32, kind="ExternalInput").ap()
    bet_d = nc.dram_tensor("bet", [128, E * 8], F32, kind="ExternalInput").ap()
    cen_d = nc.dram_tensor("cen", [128, 64], F32, kind="ExternalInput").ap()
    idf_d = nc.dram_tensor("idf", [128, 128], F32, kind="ExternalInput").ap()
    idb_d = nc.dram_tensor("idb", [128, 128], BF16, kind="ExternalInput").ap()
    mkb_d = nc.dram_tensor("mkb", [128, 896], BF16, kind="ExternalInput").ap()
    one_d = nc.dram_tensor("ones", [1, 64], F32, kind="ExternalInput").ap()
    lt_d = nc.dram_tensor("lt", [128, 128], F32, kind="ExternalInput").ap()
    o1r_d = nc.dram_tensor("o1r", [1, 128], F32, kind="ExternalInput").ap()
    o1c_d = nc.dram_tensor("o1c", [128, 1], F32, kind="ExternalInput").ap()
    cvec_d = nc.dram_tensor("cvec", [1, 8], F32, kind="ExternalInput").ap()
    p2ti_d = nc.dram_tensor("p2ti", [NSLOT, 128], F32, kind="ExternalInput").ap()
    tokf_d = nc.dram_tensor("tokf", [128, NT], F32, kind="ExternalInput").ap()
    out_d = nc.dram_tensor("out", [B_CORE, T, D], F32, kind="ExternalOutput").ap()
    ln2_dram = nc.dram_tensor("ln2sc", [T + 1, D], BF16).ap()
    del_dram = nc.dram_tensor("delsc", [T + 1, D], BF16).ap()
    p2t_dram = nc.dram_tensor("p2tsc", [NSLOT, 128], F32).ap()
    s_dram = nc.dram_tensor("ssc", [T, D], BF16).ap()
    dbg_t = {}

    bufA = nc.alloc_sbuf_tensor("bufA", [128, 16384], BF16)
    bufB = nc.alloc_sbuf_tensor("bufB", [128, 16384], BF16)
    bufC = nc.alloc_sbuf_tensor("bufC", [128, 16 * 16 * 80], BF16)
    bufD = nc.alloc_sbuf_tensor("bufD", [128, 16384], BF16)
    bufE = nc.alloc_sbuf_tensor("bufE", [128, 4096], BF16)
    wbuf = nc.alloc_sbuf_tensor("wbuf", [128, 16384], BF16)
    mkb = nc.alloc_sbuf_tensor("mkb_s", [128, 896], BF16)
    idf = nc.alloc_sbuf_tensor("idf_s", [128, 128], F32)
    idb = nc.alloc_sbuf_tensor("idb_s", [128, 128], BF16)
    onesc = nc.alloc_sbuf_tensor("ones_s", [1, 64], F32)
    ltsb = nc.alloc_sbuf_tensor("lt_s", [128, 128], F32)
    o1r = nc.alloc_sbuf_tensor("o1r_s", [1, 128], F32)
    o1c = nc.alloc_sbuf_tensor("o1c_s", [128, 1], F32)
    cvecs = nc.alloc_sbuf_tensor("cvec_s", [1, 8], F32)
    cumrow = nc.alloc_sbuf_tensor("cum_s", [1, 8], F32)
    rtsb = nc.alloc_sbuf_tensor("rt_s", [128, 8], F32)
    selr = nc.alloc_sbuf_tensor("selr_s", [128, 8], F32)
    dest_f = nc.alloc_sbuf_tensor("destf", [128, NT], F32)
    dest_i = nc.alloc_sbuf_tensor("desti", [128, NT], I32)
    tokf_sb = nc.alloc_sbuf_tensor("tokf_s", [128, NT], F32)
    pmcm = nc.alloc_sbuf_tensor("pmcm", [NSLOT, 128], F32)
    posm_f = nc.alloc_sbuf_tensor("posmf", [128, NSLOT], F32)
    posm_i = nc.alloc_sbuf_tensor("posmi", [128, NSLOT], I32)
    dtok2 = nc.alloc_sbuf_tensor("dtok2", [128, 1024], BF16)
    b1sb = nc.alloc_sbuf_tensor("b1_s", [128, E * 32], F32)
    b2sb = nc.alloc_sbuf_tensor("b2_s", [128, E * 8], F32)
    gamsb = nc.alloc_sbuf_tensor("gam_s", [128, E * 8], F32)
    betsb = nc.alloc_sbuf_tensor("bet_s", [128, E * 8], F32)
    cenT = nc.alloc_sbuf_tensor("cen_s", [128, 64], F32)
    xf = nc.alloc_sbuf_tensor("xf", [128, 2048], F32)
    tfA = nc.alloc_sbuf_tensor("tfA", [128, 1024], F32)
    tfA2 = nc.alloc_sbuf_tensor("tfA2", [128, 1024], F32)
    dtok = nc.alloc_sbuf_tensor("dtok", [128, 1024], BF16)
    scsb = nc.alloc_sbuf_tensor("scsb", [128, 16], F32)
    mks = nc.alloc_sbuf_tensor("mks", [128, 128], F32)
    lrep = nc.alloc_sbuf_tensor("lrep", [64, 1024], BF16)
    STN = ("s1", "s2", "mu1", "va", "rstd1", "rtmp", "std1", "s1b", "s2b",
           "mu2", "sq2", "c1", "c2", "tq", "mmax", "ea", "alpha")
    WIDE = ("mu1", "rstd1", "alpha", "c1", "c2")
    st = {k: nc.alloc_sbuf_tensor("st_" + k, [128, 2 * NT if k in WIDE else NT], F32)
          for k in STN}
    eps_sb = nc.alloc_sbuf_tensor("eps_s", [128, 1], F32)

    psSa = nc.alloc_psum_tensor("psSa", [128, 1024], F32)
    psSb = nc.alloc_psum_tensor("psSb", [128, 1024], F32)
    psE = nc.alloc_psum_tensor("psE", [128, 512], F32)
    psF = nc.alloc_psum_tensor("psF", [128, 512], F32)
    psT = nc.alloc_psum_tensor("psT", [128, 2048], BF16)

    P = Prog()
    SX = [P.sems["sx0"], P.sems["sx1"]]
    SO, SWC = P.sems["so"], P.sems["swc"]
    SWH = [P.sems["swh0"], P.sems["swh1"]]
    SW1 = [P.sems["sw10"], P.sems["sw11"]]
    SW2 = [P.sems["sw20"], P.sems["sw21"]]
    SP2T, SDEL, SIDX, SPM, SDS = (P.sems["sp2t"], P.sems["sdel"],
                                  P.sems["sidx"], P.sems["spm"], P.sems["sds"])
    SLN = [P.sems["sln0"], P.sems["sln1"]]
    SDL = [P.sems["sdl0"], P.sems["sdl1"]]
    SSD = [P.sems["ssd%d" % i] for i in range(4)]

    bA, bB, bC, bD, bE = Buf("A"), Buf("B"), Buf("C"), Buf("D"), Buf("E")
    bA4 = [Buf("A4_%d" % i) for i in range(4)]
    bCln, bCh = Buf("Cln"), Buf("Ch")
    bW0, bW1 = Buf("W0"), Buf("W1")
    bw1s = [Buf("w1s0"), Buf("w1s1")]
    bw2s = [Buf("w2s0"), Buf("w2s1")]
    bPT = [Buf("PT0"), Buf("PT1"), Buf("PT2"), Buf("PT3")]
    bXF = [Buf("xf0"), Buf("xf1")]
    bTfA, bTfA2, bDtok = Buf("tfA"), Buf("tfA2"), Buf("dtok")
    bSc = [Buf("sc0"), Buf("sc1")]
    bMk = Buf("mks")
    bCum, bRt, bSelr = Buf("cum"), Buf("rt"), Buf("selr")
    bDest, bDestI = Buf("dest"), Buf("desti")
    bP2T, bLn2D, bDelD = Buf("p2t"), Buf("ln2d"), Buf("deld")
    bPmcm, bPosmF, bPosmI = Buf("pmcm"), Buf("posmf"), Buf("posmi")
    bSd = Buf("sdram")
    bGb = [Buf("gb0"), Buf("gb1"), Buf("gb2"), Buf("gb3")]
    bGhT, bHt2, bDtT, bZl = Buf("ghT"), Buf("ht2"), Buf("dtT"), Buf("zl")
    bDt2 = Buf("dtok2")
    bLrep = Buf("lrep")
    bST = {k: Buf("st_" + k) for k in STN}
    bConst = Buf("const")
    bSa0, bSa1 = Buf("psSa0"), Buf("psSa1")
    bSb0, bSb1 = Buf("psSb0"), Buf("psSb1")
    bPsE, bPsF = Buf("psE"), Buf("psF")
    bPsT = [Buf("psT0"), Buf("psT1")]
    bOutT = Buf("outT")

    mm1rot = [(psSa, 0, bSa0), (psSa, 512, bSa1), (psSb, 0, bSb0), (psSb, 512, bSb1)]

    def dma(eng, dst_ap, src_ap, wbufs, rbufs, sem):
        P.emit(eng, lambda EG, d=dst_ap, s=src_ap: EG.dma_start(out=d, in_=s),
               reads=rbufs, writes=wbufs, sem=sem)

    def dump(name, ap, shape, dtype, rbufs):
        if not debug:
            return
        dd = nc.dram_tensor(name, list(shape), dtype, kind="ExternalOutput").ap()
        dbg_t[name] = dd
        dma("sp", dd, ap, [], rbufs, P.sems["so"])

    for ci, (dst_t, src) in enumerate(
            ((mkb, mkb_d), (idf, idf_d), (idb, idb_d),
             (onesc, one_d), (b1sb, b1_d), (b2sb, b2_d),
             (gamsb, gam_d), (betsb, bet_d), (cenT, cen_d),
             (ltsb, lt_d), (o1r, o1r_d), (o1c, o1c_d), (cvecs, cvec_d),
             (tokf_sb, tokf_d))):
        P.wait("pool", SWC, 16 * ci)
        dma("pool", dst_t[:, :], src, [bConst], [], SWC)

    P.emit("dve", lambda EG: EG.memset(eps_sb[:, :], EPS), reads=[], writes=[bConst])

    def mm_group(out_ap, pairs, reads, writes):
        def fn(EG, o=out_ap, ps=tuple(pairs)):
            last = None
            n = len(ps)
            for i, (l, r) in enumerate(ps):
                last = EG.matmul(o, l, r, start=(i == 0), stop=(i == n - 1))
            return last
        return P.emit("pe", fn, reads=reads, writes=writes)

    pending_final = []
    for b in range(B_CORE):
        dma("pool", p2t_dram, p2ti_d, [bP2T], [], SP2T)
        P.emit("dve", lambda EG: EG.memset(cumrow[0:1, :], 0.0),
               reads=[], writes=[bCum])
        for t in range(NT):
            sl = t % 2
            xfs = xf[:, sl * 1024:sl * 1024 + 1024]
            bxf = bXF[sl]
            if t == 0:
                dma("sp", xfs, x_d[b, 0:128, :], [bxf], [], SX[0])
            if t + 1 < NT:
                sl2 = (t + 1) % 2
                dma("sp", xf[:, sl2 * 1024:sl2 * 1024 + 1024],
                    x_d[b, (t + 1) * 128:(t + 2) * 128, :], [bXF[sl2]], [], SX[sl2])
            P.emit("dve", lambda EG, i=xfs, o=st["s1"][:, t:t + 1]:
                   EG.reduce_sum(o, i, axis=AX.X), reads=[bxf], writes=[bST["s1"]])
            P.emit("act", lambda EG, i=xfs, a=st["s2"][:, t:t + 1]:
                   EG.activation(bufE[:, 0:1024], i, AF.Square, accum_out=a),
                   reads=[bxf], writes=[bPT[0], bST["s2"]])
            bc0 = b * NT
            P.emit("act", lambda EG, t=t, bc0=b * NT: EG.mul(st["mu1"][:, bc0 + t:bc0 + t + 1],
                                                 st["s1"][:, t:t + 1], 1.0 / D),
                   reads=[bST["s1"]], writes=[bST["mu1"]])
            P.emit("pool", lambda EG, t=t, bc0=b * NT: EG.tensor_tensor(
                st["tq"][:, t:t + 1], st["mu1"][:, bc0 + t:bc0 + t + 1],
                st["mu1"][:, bc0 + t:bc0 + t + 1], ALU.mult),
                reads=[bST["mu1"]], writes=[bST["tq"]])
            P.emit("dve", lambda EG, t=t: EG.scalar_tensor_tensor(
                st["va"][:, t:t + 1], st["s2"][:, t:t + 1], 1.0 / D,
                st["tq"][:, t:t + 1], ALU.mult, ALU.subtract),
                reads=[bST["s2"], bST["tq"]], writes=[bST["va"]])
            P.emit("act", lambda EG, t=t: EG.activation(
                st["std1"][:, t:t + 1], st["va"][:, t:t + 1], AF.Sqrt,
                bias=eps_sb[:, 0:1]),
                reads=[bST["va"], bConst], writes=[bST["std1"]])
            P.emit("dve", lambda EG, t=t: EG.reciprocal(
                st["rtmp"][:, t:t + 1], st["std1"][:, t:t + 1]),
                reads=[bST["std1"]], writes=[bST["rtmp"]])
            P.emit("pool", lambda EG, t=t, bc0=b * NT: EG.tensor_copy(
                st["rstd1"][:, bc0 + t:bc0 + t + 1], st["rtmp"][:, t:t + 1]),
                reads=[bST["rtmp"]], writes=[bST["rstd1"]])
            tfX = tfA if t % 2 == 0 else tfA2
            bTfX = bTfA if t % 2 == 0 else bTfA2
            P.emit("dve", lambda EG, i=xfs, t=t, o=tfX, bc0=b * NT:
                   EG.tensor_scalar(o[:, :], i, st["mu1"][:, bc0 + t:bc0 + t + 1],
                                    st["rstd1"][:, bc0 + t:bc0 + t + 1],
                                    ALU.subtract, ALU.mult),
                   reads=[bxf, bST["mu1"], bST["rstd1"]], writes=[bTfX])
            dslot0 = dtok if t % 2 == 0 else dtok2
            bds0 = bDtok if t % 2 == 0 else bDt2
            P.emit("act", lambda EG, i=tfX, o=dslot0: EG.copy(o[:, :], i[:, :]),
                   reads=[bTfX], writes=[bds0])
            P.emit("dve", lambda EG, o=st["s1b"][:, t:t + 1], i=tfX:
                   EG.reduce_sum(o, i[:, :], axis=AX.X),
                   reads=[bTfX], writes=[bST["s1b"]])
            P.emit("act", lambda EG, a=st["s2b"][:, t:t + 1], i=tfX:
                   EG.activation(bufE[:, 0:1024], i[:, :], AF.Square, accum_out=a),
                   reads=[bTfX], writes=[bPT[0], bST["s2b"]])
            P.emit("act", lambda EG, t=t: EG.mul(st["mu2"][:, t:t + 1],
                                                 st["s1b"][:, t:t + 1], 1.0 / D),
                   reads=[bST["s1b"]], writes=[bST["mu2"]])
            P.emit("pool", lambda EG, t=t: EG.tensor_tensor(
                st["tq"][:, t:t + 1], st["mu2"][:, t:t + 1],
                st["mu2"][:, t:t + 1], ALU.mult),
                reads=[bST["mu2"]], writes=[bST["tq"]])
            P.emit("dve", lambda EG, t=t: EG.scalar_tensor_tensor(
                st["va"][:, t:t + 1], st["s2b"][:, t:t + 1], 1.0 / D,
                st["tq"][:, t:t + 1], ALU.mult, ALU.subtract),
                reads=[bST["s2b"], bST["tq"]], writes=[bST["va"]])
            P.emit("act", lambda EG, t=t: EG.activation(
                st["sq2"][:, t:t + 1], st["va"][:, t:t + 1], AF.Sqrt,
                bias=eps_sb[:, 0:1]),
                reads=[bST["va"], bConst], writes=[bST["sq2"]])
            P.emit("pool", lambda EG, t=t: EG.tensor_tensor(
                st["ea"][:, t:t + 1], st["std1"][:, t:t + 1],
                st["sq2"][:, t:t + 1], ALU.mult),
                reads=[bST["std1"], bST["sq2"]], writes=[bST["ea"]])
            P.emit("dve", lambda EG, t=t, bc0=b * NT: EG.reciprocal(
                st["c2"][:, bc0 + t:bc0 + t + 1], st["ea"][:, t:t + 1]),
                reads=[bST["ea"]], writes=[bST["c2"]])
            P.emit("dve", lambda EG, t=t, bc0=b * NT: EG.scalar_tensor_tensor(
                st["c1"][:, bc0 + t:bc0 + t + 1], st["mu2"][:, t:t + 1],
                st["std1"][:, t:t + 1],
                st["mu1"][:, bc0 + t:bc0 + t + 1], ALU.mult, ALU.add),
                reads=[bST["mu2"], bST["std1"], bST["mu1"]], writes=[bST["c1"]])
            half = t % 2
            pshalf = psT[:, half * 1024:half * 1024 + 1024]
            def _trb(EG, ph=pshalf, i=dslot0):
                last = None
                for c in range(8):
                    last = EG.transpose(ph[:, c * 128:(c + 1) * 128],
                                        i[:, c * 128:(c + 1) * 128], idb[:, :])
                return last
            P.emit("pe", _trb, reads=[bds0, bConst], writes=[bPsT[half]])
            dstA = bufA[:, 0:8 * T].rearrange("p (c q) -> p c q", c=8)[:, :, t * 128:(t + 1) * 128]
            srcT = pshalf.rearrange("p (c q) -> p c q", c=8)
            P.emit("dve", lambda EG, d2=dstA, s2=srcT: EG.tensor_copy(d2, s2),
                   reads=[bPsT[half]], writes=[bA4[t // 4]])
            for c in range(8):
                ps, pb = (psE, bPsE) if c % 2 == 0 else (psF, bPsF)
                P.emit("pe", lambda EG, c=c, ps=ps, i=tfX:
                       EG.transpose(ps[:, 0:128], i[:, c * 128:(c + 1) * 128], idf[:, :]),
                       reads=[bTfX, bConst], writes=[pb])
                P.emit("act", lambda EG, c=c, ps=ps, o=xfs:
                       EG.copy(o[:, c * 128:(c + 1) * 128], ps[:, 0:128]),
                       reads=[pb], writes=[bxf])
            mm_group(psSa[:, 0:8],
                     [(xfs[:, c * 128:(c + 1) * 128], cenT[:, c * 8:(c + 1) * 8])
                      for c in range(8)],
                     reads=[bxf, bConst], writes=[bSa0])
            ssl = scsb[:, sl * 8:sl * 8 + 8]
            msl = mks[:, t * 8:t * 8 + 8]
            P.emit("act", lambda EG, o=ssl: EG.copy(o, psSa[:, 0:8]),
                   reads=[bSa0], writes=[bSc[sl]])
            P.emit("dve", lambda EG, i=ssl, t=t:
                   EG.reduce_max(st["mmax"][:, t:t + 1], i, axis=AX.X),
                   reads=[bSc[sl]], writes=[bST["mmax"]])
            P.emit("pool", lambda EG, i=ssl, o=msl, t=t:
                   EG.tensor_single_scalar(o, i, st["mmax"][:, t:t + 1], ALU.is_equal),
                   reads=[bSc[sl], bST["mmax"]], writes=[bMk])
            def _rank(EG, m=msl):
                EG.matmul(psSb[:, 0:8], ltsb[:, :], m, start=True, stop=False)
                EG.matmul(psSb[:, 0:8], o1r[0:1, :], cumrow[0:1, :], start=False, stop=False)
                return EG.matmul(psSb[:, 0:8], o1r[0:1, :], cvecs[0:1, :], start=False, stop=True)
            P.emit("pe", _rank, reads=[bConst, bMk, bCum], writes=[bSb0])
            P.emit("pe", lambda EG, m=msl, t=t: EG.matmul(
                psSb[0:1, 512:520], o1c[:, :], m, start=(t == 0), stop=(t == NT - 1)),
                reads=[bConst, bMk], writes=[bSb1])
            P.emit("act", lambda EG: EG.copy(cumrow[0:1, :], psSb[0:1, 512:520]),
                   reads=[bSb1], writes=[bCum])
            P.emit("act", lambda EG: EG.copy(rtsb[:, :], psSb[:, 0:8]),
                   reads=[bSb0], writes=[bRt])
            P.emit("pool", lambda EG, m=msl: EG.tensor_tensor(selr[:, :], m, rtsb[:, :], ALU.mult),
                   reads=[bMk, bRt], writes=[bSelr])
            P.emit("dve", lambda EG, t=t: EG.reduce_sum(
                dest_f[:, t:t + 1], selr[:, :], axis=AX.X),
                reads=[bSelr], writes=[bDest])

        P.emit("act", lambda EG: EG.activation(st["ea"][:, :], st["mmax"][:, :],
                                               AF.Exp, scale=-1.0),
               reads=[bST["mmax"]], writes=[bST["ea"]])
        P.emit("pool", lambda EG: EG.tensor_scalar(
            st["tq"][:, :], st["ea"][:, :], 1.0, None, ALU.add),
            reads=[bST["ea"]], writes=[bST["tq"]])
        P.emit("dve", lambda EG, bc0=b * NT: EG.reciprocal(
            st["alpha"][:, bc0:bc0 + NT], st["tq"][:, :]),
            reads=[bST["tq"]], writes=[bST["alpha"]])
        P.emit("dve", lambda EG: EG.tensor_copy(dest_i[:, :], dest_f[:, :]),
               reads=[bDest], writes=[bDestI])
        p2t_flat = p2t_dram.rearrange("a (b c) -> (a b) c", c=1)
        for t in range(NT):
            P.emit("pool", lambda EG, t=t: EG.indirect_dma_start(
                out=p2t_flat,
                out_offset=IndirectOffsetOnAxis(ap=dest_i[:, t:t + 1], axis=0),
                in_=tokf_sb[:, t:t + 1], in_offset=None),
                reads=[bDestI, bConst, bP2T], writes=[bP2T], sem=SIDX)
        dma("pool", pmcm[:, :], p2t_dram, [bPmcm], [bP2T], SPM)
        P.emit("pe", lambda EG: EG.transpose(psE[:, 0:NSLOT], pmcm[:, :],
                                             idf[0:NSLOT, 0:NSLOT]),
               reads=[bPmcm, bConst], writes=[bPsE])
        P.emit("act", lambda EG: EG.copy(posm_f[:, :], psE[:, 0:NSLOT]),
               reads=[bPsE], writes=[bPosmF])
        P.emit("dve", lambda EG: EG.tensor_copy(posm_i[:, :], posm_f[:, :]),
               reads=[bPosmF], writes=[bPosmI])

        if debug and b == 0:
            for nm in ("mu1", "rstd1", "c1", "c2", "alpha", "mmax",
                       "s1", "s2", "va", "tq", "std1", "s1b", "s2b",
                       "mu2", "sq2", "ea"):
                dump("d_" + nm, st[nm][:, 0:NT], (128, NT), F32, [bST[nm]])
            dump("d_xnT", bufA[:, :], (128, 16384), BF16, [bA])
            dump("d_dest", dest_f[:, :], (128, NT), F32, [bDest])
            dump("d_posm", posm_f[:, :], (128, NSLOT), F32, [bPosmF])

        def load_w(dram, half, bhalf):
            for k in range(8):
                P.wait("pool", SWH[half], SWH[half].v)
                dma("pool",
                    wbuf[:, half * 8192 + k * 1024: half * 8192 + (k + 1) * 1024],
                    dram[k * 128:(k + 1) * 128, :], [bhalf], [], SWH[half])
        load_w(wq_d, 0, bW0)
        load_w(wk_d, 1, bW1)

        for (wsl, bws, dstbuf, bdst) in ((0, bW0, bufD, bD), (1, bW1, bufB, bB)):
            for c in range(8):
                for q4 in range(4):
                    pst, off, bbank = mm1rot[(c * 4 + q4) % 4]
                    out_ap = pst[:, off:off + 512]
                    pairs = [(wbuf[:, wsl * 8192 + k * 1024 + c * 128:
                                   wsl * 8192 + k * 1024 + c * 128 + 128],
                              bufA[:, k * T + q4 * 512: k * T + q4 * 512 + 512])
                             for k in range(8)]
                    mm_group(out_ap, pairs, reads=[bws, bA4[q4]], writes=[bbank])
                    P.emit("dve", lambda EG,
                           o=dstbuf[:, c * T + q4 * 512: c * T + q4 * 512 + 512],
                           i=out_ap: EG.tensor_copy(o, i),
                           reads=[bbank], writes=[bdst])
                    if pending_final:
                        pending_final.pop(0)()
                    if pending_final:
                        pending_final.pop(0)()
        ones_ap = bufC[:, 0:16 * 16 * VS].rearrange(
            "p (kt h j) -> p kt h j", h=H, j=VS)[:, :, :, 64:65]
        P.emit("dve", lambda EG, a=ones_ap: EG.memset(a, 1.0),
               reads=[], writes=[bC])
        load_w(wv_d, 0, bW0)
        for kt in range(16):
            for hf in range(2):
                pst, off, bbank = mm1rot[(kt * 2 + hf) % 4]
                out_ap = pst[:, off:off + 512]
                pairs = [(bufA[:, k * T + kt * 128: k * T + (kt + 1) * 128],
                          wbuf[:, k * 1024 + hf * 512: k * 1024 + hf * 512 + 512])
                         for k in range(8)]
                mm_group(out_ap, pairs, reads=[bA4[kt // 4], bW0], writes=[bbank])
                if pending_final:
                    pending_final.pop(0)()
                if pending_final:
                    pending_final.pop(0)()
                dstV = bufC[:, kt * 1280 + hf * 8 * VS: kt * 1280 + (hf + 1) * 8 * VS]\
                    .rearrange("p (h j) -> p h j", j=VS)[:, :, 0:64]
                srcV = out_ap.rearrange("p (h j) -> p h j", j=64)
                P.emit("dve", lambda EG, d2=dstV, s2=srcV: EG.tensor_copy(d2, s2),
                       reads=[bbank], writes=[bC])

        if debug and b == 0:
            dump("d_qt", bufD[:, :], (128, 16384), BF16, [bD])
            dump("d_kt", bufB[:, :], (128, 16384), BF16, [bB])
            dump("d_vp", bufC[:, :], (128, 16 * 16 * 80), BF16, [bC])

        for _pt in bPT:
            _pt.merge_from(bE)
        bA.merge_from(*bA4)
        load_w(wo_d, 1, bW1)

        while pending_final:
            pending_final.pop(0)()

        pending_tail = [None]

        def emit_tail(hp, qg):
            def tail():
                def _ln(EG):
                    EG.activation(tfA[32:33, 0:512], psE[64:65, :], AF.Ln)
                    return EG.activation(tfA[32:33, 512:1024], psF[64:65, :], AF.Ln)
                P.emit("act", _ln, reads=[bPsE, bPsF], writes=[bTfA])
                P.emit("act", lambda EG: EG.activation(
                    tfA[0:1, 0:1024], tfA[32:33, 0:1024], AF.Exp, scale=-1.0),
                    reads=[bTfA], writes=[bTfA])
                def _lbc(EG):
                    EG.matmul(psE[64:128, :], onesc[0:1, :], tfA[0:1, 0:512],
                              start=True, stop=True)
                    return EG.matmul(psF[64:128, :], onesc[0:1, :],
                                     tfA[0:1, 512:1024], start=True, stop=True)
                P.emit("pe", _lbc, reads=[bTfA, bConst], writes=[bPsE, bPsF])
                def _lrp(EG):
                    EG.copy(lrep[0:64, 0:512], psE[64:128, :])
                    return EG.copy(lrep[0:64, 512:1024], psF[64:128, :])
                P.emit("act", _lrp, reads=[bPsE, bPsF], writes=[bLrep])
                def _cev(EG, hp=hp, qg=qg):
                    EG.tensor_tensor(
                        bufA[0:64, hp * T + qg * 512: hp * T + qg * 512 + 512],
                        psE[0:64, :], lrep[0:64, 0:512], ALU.mult)
                    return EG.tensor_tensor(
                        bufA[64:128, hp * T + qg * 512: hp * T + qg * 512 + 512],
                        psF[0:64, :], lrep[0:64, 512:1024], ALU.mult)
                P.emit("dve", _cev, reads=[bPsE, bPsF, bLrep], writes=[bA])
            return tail

        for hp in range(8):
            for qg in (3, 2, 1, 0):
                ktmax = 4 * qg + 4

                def emit_S(kt, hp=hp, qg=qg):
                    pair = psSb if kt % 2 == 0 else psSa
                    bp = (bSb0, bSb1) if kt % 2 == 0 else (bSa0, bSa1)
                    def _sp(EG, pp=pair, hp=hp, qg=qg, kt=kt):
                        EG.matmul(pp[:, 0:512],
                                  bufB[0:64, hp * T + kt * 128: hp * T + (kt + 1) * 128],
                                  bufD[0:64, hp * T + qg * 512: hp * T + qg * 512 + 512],
                                  start=True, stop=True, tile_position=(0, 0))
                        return EG.matmul(
                            pp[:, 512:1024],
                            bufB[64:128, hp * T + kt * 128: hp * T + (kt + 1) * 128],
                            bufD[64:128, hp * T + qg * 512: hp * T + qg * 512 + 512],
                            start=True, stop=True, tile_position=(64, 0))
                    P.emit("pe", _sp, reads=[bB, bD], writes=list(bp))
                    sl = kt % 4
                    P.emit("act", lambda EG, pp=pair, sl=sl:
                           EG.activation(bufE[:, sl * 1024: sl * 1024 + 1024],
                                         pp[:, 0:1024], AF.Exp, scale=0.125),
                           reads=list(bp), writes=[bPT[sl]])
                    j = kt - 4 * qg
                    if j >= 0:
                        def _mask(EG, sl=sl, j=j):
                            EG.tensor_tensor(bufE[:, sl * 1024: sl * 1024 + 512],
                                             bufE[:, sl * 1024: sl * 1024 + 512],
                                             mkb[:, 384 - 128 * j: 896 - 128 * j], ALU.mult)
                            return EG.tensor_tensor(
                                bufE[:, sl * 1024 + 512: sl * 1024 + 1024],
                                bufE[:, sl * 1024 + 512: sl * 1024 + 1024],
                                mkb[:, 384 - 128 * j: 896 - 128 * j], ALU.mult)
                        P.emit("dve", _mask, reads=[bConst], writes=[bPT[sl]])

                def emit_ctx(kt, hp=hp, ktmax=ktmax):
                    sl = kt % 4
                    def _ctx(EG, hp=hp, kt=kt, sl=sl,
                             first=(kt == 0), last=(kt == ktmax - 1)):
                        EG.matmul(psE[0:65, :],
                                  bufC[:, kt * 1280 + (2 * hp) * VS:
                                       kt * 1280 + (2 * hp) * VS + 65],
                                  bufE[:, sl * 1024: sl * 1024 + 512],
                                  start=first, stop=last)
                        return EG.matmul(psF[0:65, :],
                                         bufC[:, kt * 1280 + (2 * hp + 1) * VS:
                                              kt * 1280 + (2 * hp + 1) * VS + 65],
                                         bufE[:, sl * 1024 + 512: sl * 1024 + 1024],
                                         start=first, stop=last)
                    P.emit("pe", _ctx, reads=[bC, bPT[sl]], writes=[bPsE, bPsF])

                for kt in range(ktmax):
                    emit_S(kt)
                    if kt == 2 and pending_tail[0] is not None:
                        pending_tail[0]()
                        pending_tail[0] = None
                    if kt >= 3:
                        emit_ctx(kt - 3)
                if pending_tail[0] is not None:
                    pending_tail[0]()
                    pending_tail[0] = None
                for kt in range(max(0, ktmax - 3), ktmax):
                    emit_ctx(kt)
                pending_tail[0] = emit_tail(hp, qg)
        pending_tail[0]()
        pending_tail[0] = None

        if debug and b == 0:
            dump("d_ctx", bufA[:, :], (128, 16384), BF16, [bA])

        for qt in range(16):
            for dh in range(2):
                pst, pb = (psE, bPsE) if (qt * 2 + dh) % 2 == 0 else (psF, bPsF)
                pairs = [(bufA[:, c * T + qt * 128: c * T + (qt + 1) * 128],
                          wbuf[:, 8192 + c * 1024 + dh * 512:
                               8192 + c * 1024 + dh * 512 + 512])
                         for c in range(8)]
                mm_group(pst[:, :], pairs, reads=[bA, bW1], writes=[pb])
                ssl_ = qt % 4
                sap = bufE[:, ssl_ * 1024 + dh * 512: ssl_ * 1024 + dh * 512 + 512]
                P.emit("dve", lambda EG, o=sap, i=pst[:, :]: EG.tensor_copy(o, i),
                       reads=[pb], writes=[bPT[ssl_]])
                dma("sp", s_dram[qt * 128:(qt + 1) * 128, dh * 512:dh * 512 + 512],
                    sap, [bSd], [bPT[ssl_]], SSD[ssl_])

        if debug and b == 0:
            dump("d_s", bufB[:, :], (128, 16384), BF16, [bB])

        bZl.merge_from(bC)
        bGb[0].merge_from(bC)
        bGb[1].merge_from(bC)
        bGhT.merge_from(bC)
        bHt2.merge_from(bC)
        bDtT.merge_from(bC)
        for s_ in bw1s + bw2s:
            s_.merge_from(bW0, bW1)

        P.emit("dve", lambda EG: EG.memset(bufC[:, 0:1024], 0.0),
               reads=[], writes=[bGb[0]])
        for t in range(NT):
            sl = t % 2
            xfs = xf[:, sl * 1024:sl * 1024 + 1024]
            bxf = bXF[sl]
            if t == 0:
                dma("sp", xfs, x_d[b, 0:128, :], [bxf], [], SX[0])
            if t + 1 < NT:
                sl2 = (t + 1) % 2
                dma("sp", xf[:, sl2 * 1024:sl2 * 1024 + 1024],
                    x_d[b, (t + 1) * 128:(t + 2) * 128, :], [bXF[sl2]], [], SX[sl2])
            dslot = dtok if sl == 0 else dtok2
            bdsl = bDtok if sl == 0 else bDt2
            P.emit("dve", lambda EG, i=xfs, t=t, o=dslot, bc0=b * NT:
                   EG.tensor_scalar(o[:, :], i, st["c1"][:, bc0 + t:bc0 + t + 1],
                                    st["c2"][:, bc0 + t:bc0 + t + 1],
                                    ALU.subtract, ALU.mult),
                   reads=[bxf, bST["c1"], bST["c2"]], writes=[bdsl])
            dma("sp", ln2_dram[t * 128:(t + 1) * 128, :], dslot[:, :],
                [bLn2D], [bdsl], SLN[sl])
            dma("sp", del_dram[t * 128:(t + 1) * 128, :], bufC[:, 0:1024],
                [bDelD], [bGb[0]], SDEL)

        w1q = [(e_, fs_) for e_ in range(E) for fs_ in range(16)]
        w2q = [(e_, dc_) for e_ in range(E) for dc_ in range(8)]
        wp = {"w1": 0, "w2": 0}
        def pf1():
            if wp["w1"] < len(w1q):
                e_, fs_ = w1q[wp["w1"]]
                wp["w1"] += 1
                sl_ = fs_ % 2
                dma("pool", wbuf[:, sl_ * 2048:(sl_ + 1) * 2048],
                    w1_d[e_, fs_], [bw1s[sl_]], [], SW1[sl_])
        def pf2():
            if wp["w2"] < len(w2q):
                e_, dc_ = w2q[wp["w2"]]
                wp["w2"] += 1
                sl_ = dc_ % 2
                dma("pool", wbuf[:, 4096 + sl_ * 4096: 4096 + (sl_ + 1) * 4096],
                    w2_d[e_, dc_], [bw2s[sl_]], [], SW2[sl_])
        pf1(); pf1(); pf2(); pf2()

        bGh = [bGhT, bHt2]
        dtT = bufC[:, 12288:12288 + 8 * CAP].rearrange("p (c q) -> p c q", c=8)

        defq = []
        def gather_emit(e):
            par = e % 2
            for g in range(NG):
                gb = bufC[:, g * 1024:(g + 1) * 1024]
                col = e * NG + g
                P.emit("pool", lambda EG, gb=gb, col=col: EG.indirect_dma_start(
                    out=gb, out_offset=None,
                    in_=ln2_dram,
                    in_offset=IndirectOffsetOnAxis(ap=posm_i[:, col:col + 1], axis=0)),
                    reads=[bPosmI, bLn2D], writes=[bGb[g]],
                    sem=P.sems["sg%d" % g])
        def defer_gather_tr(e):
            par = e % 2
            gbase = 4096 + par * 4096
            for g in range(NG):
                def emit_tr(g=g, gbase=gbase, par=par):
                    gb = bufC[:, g * 1024:(g + 1) * 1024]
                    half = g % 2
                    pshalf = psT[:, half * 1024:half * 1024 + 1024]
                    def _trg(EG, gb=gb, ph=pshalf):
                        last = None
                        for c in range(8):
                            last = EG.transpose(ph[:, c * 128:(c + 1) * 128],
                                                gb[:, c * 128:(c + 1) * 128], idb[:, :])
                        return last
                    P.emit("pe", _trg, reads=[bGb[g], bConst], writes=[bPsT[half]])
                    src_t = pshalf.rearrange("p (c q) -> p c q", c=8)
                    dst_t = bufC[:, gbase:gbase + 8 * CAP].rearrange(
                        "p (c q) -> p c q", c=8)[:, :, g * 128:(g + 1) * 128]
                    P.emit("dve", lambda EG, d2=dst_t, s2=src_t: EG.tensor_copy(d2, s2),
                           reads=[bPsT[half]], writes=[bGh[par]])
                defq.append(emit_tr)
            def emit_ht(e=e, gbase=gbase, par=par):
                def _ht(EG, e=e, gbase=gbase):
                    last = None
                    for c in range(8):
                        sl_ = bufC[:, gbase + c * CAP: gbase + (c + 1) * CAP]
                        last = EG.tensor_scalar(
                            sl_, sl_,
                            gamsb[:, e * 8 + c: e * 8 + c + 1],
                            betsb[:, e * 8 + c: e * 8 + c + 1],
                            ALU.mult, ALU.add)
                    return last
                P.emit("dve", _ht, reads=[bGh[par], bConst], writes=[bGh[par]])
            defq.append(emit_ht)
        def defer_back_tr(e):
            par = e % 2
            for g in range(NG):
                def emit_bt(g=g, e=e):
                    half = g % 2
                    pshalf = psT[:, half * 1024:half * 1024 + 1024]
                    def _trd(EG, ph=pshalf, g=g):
                        last = None
                        for c in range(8):
                            last = EG.transpose(ph[:, c * 128:(c + 1) * 128],
                                                dtT[:, c, g * 128:(g + 1) * 128], idb[:, :])
                        return last
                    P.emit("pe", _trd, reads=[bDtT, bConst], writes=[bPsT[half]])
                    dslot = dtok if g % 2 == 0 else dtok2
                    bdsl = bDtok if g % 2 == 0 else bDt2
                    P.emit("dve", lambda EG, ph=pshalf, o=dslot: EG.tensor_copy(o[:, :], ph),
                           reads=[bPsT[half]], writes=[bdsl])
                    col = e * NG + g
                    P.emit("pool", lambda EG, o=dslot, col=col: EG.indirect_dma_start(
                        out=del_dram, out_offset=IndirectOffsetOnAxis(
                            ap=posm_i[:, col:col + 1], axis=0),
                        in_=o[:, :], in_offset=None),
                        reads=[bPosmI, bdsl, bDelD], writes=[bDelD], sem=SDS)
                defq.append(emit_bt)

        gather_emit(0)
        defer_gather_tr(0)
        while defq:
            defq.pop(0)()
        for e in range(E):
            par = e % 2
            gbase = 4096 + par * 4096
            if e + 1 < E:
                gather_emit(e + 1)
                defer_gather_tr(e + 1)
            for fc in range(32):
                fs, fc2 = fc // 2, fc % 2
                wsl = fs % 2
                pst, off, bbank = mm1rot[fc % 4]
                out_ap = pst[:, off:off + CAP]
                pairs = [(wbuf[:, wsl * 2048 + dcc * 256 + fc2 * 128:
                               wsl * 2048 + dcc * 256 + fc2 * 128 + 128],
                          bufC[:, gbase + dcc * CAP: gbase + (dcc + 1) * CAP])
                         for dcc in range(8)]
                mm_group(out_ap, pairs, reads=[bw1s[wsl], bGh[par]], writes=[bbank])
                P.emit("act", lambda EG, o=bufD[:, fc * CAP:(fc + 1) * CAP], i=out_ap,
                       bc=b1sb[:, e * 32 + fc: e * 32 + fc + 1]:
                       EG.activation(o, i, AF.Relu, bias=bc),
                       reads=[bbank, bConst], writes=[bD])
                if fc % 2 == 1:
                    pf1()
                if fc % 3 == 2 and defq:
                    defq.pop(0)()

            for dc in range(8):
                wsl = dc % 2
                pst, pb = (psE, bPsE) if dc % 2 == 0 else (psF, bPsF)
                pairs = [(wbuf[:, 4096 + wsl * 4096 + fc * 128:
                               4096 + wsl * 4096 + (fc + 1) * 128],
                          bufD[:, fc * CAP:(fc + 1) * CAP])
                         for fc in range(32)]
                mm_group(pst[:, 0:CAP], pairs, reads=[bw2s[wsl], bD], writes=[pb])
                P.emit("act", lambda EG, o=bufC[:, 12288 + dc * CAP: 12288 + (dc + 1) * CAP],
                       i=pst[:, 0:CAP], bc=b2sb[:, e * 8 + dc: e * 8 + dc + 1]:
                       EG.activation(o, i, AF.Identity, bias=bc),
                       reads=[pb, bConst], writes=[bDtT])
                pf2()

            defer_back_tr(e)
        while defq:
            defq.pop(0)()

        def final_ops(b=b):
            ops = []
            for t in range(NT):
                def mk(t=t, b=b):
                    sl = t % 2
                    xfs = xf[:, sl * 1024:sl * 1024 + 1024]
                    bxf = bXF[sl]
                    dslot = dtok if sl == 0 else dtok2
                    bdsl = bDtok if sl == 0 else bDt2
                    tfX = tfA if t % 2 == 0 else tfA2
                    bTfX = bTfA if t % 2 == 0 else bTfA2
                    ssl_ = t % 4
                    sap = bufE[:, ssl_ * 1024: ssl_ * 1024 + 1024]
                    steps = []
                    steps.append(lambda: dma("sp", xfs,
                                             x_d[b, t * 128:(t + 1) * 128, :],
                                             [bxf], [], SX[sl]))
                    steps.append(lambda: dma("sp", dslot[:, :],
                                             del_dram[t * 128:(t + 1) * 128, :],
                                             [bdsl], [bDelD], SDL[sl]))
                    steps.append(lambda: dma("sp", sap,
                                             s_dram[t * 128:(t + 1) * 128, :],
                                             [bPT[ssl_]], [bSd], SSD[ssl_]))
                    def _fc1(EG, i=xfs, t=t, ds=dslot, o=tfX, bc0=b * NT):
                        EG.tensor_scalar(o[:, :], i,
                                         st["mu1"][:, bc0 + t:bc0 + t + 1],
                                         st["rstd1"][:, bc0 + t:bc0 + t + 1],
                                         ALU.subtract, ALU.mult)
                        return EG.scalar_tensor_tensor(
                            o[:, :], ds[:, :],
                            st["alpha"][:, bc0 + t:bc0 + t + 1],
                            o[:, :], ALU.mult, ALU.add)
                    steps.append(lambda: P.emit(
                        "dve", _fc1,
                        reads=[bxf, bdsl, bST["mu1"], bST["rstd1"], bST["alpha"]],
                        writes=[bTfX]))
                    steps.append(lambda: P.emit(
                        "act", lambda EG, ds=dslot, i=tfX:
                        EG.activation(ds[:, :], i[:, :], AF.Silu),
                        reads=[bTfX], writes=[bdsl]))
                    def _fc2(EG, i=xfs, ds=dslot, o=tfX, sp2=sap):
                        EG.tensor_tensor(o[:, :], ds[:, :], sp2, ALU.mult)
                        return EG.tensor_tensor(o[:, :], i, o[:, :], ALU.add)
                    steps.append(lambda: P.emit(
                        "dve", _fc2, reads=[bxf, bdsl, bPT[ssl_]],
                        writes=[bTfX, bOutT]))
                    steps.append(lambda: dma("sp", out_d[b, t * 128:(t + 1) * 128, :],
                                             tfX[:, :], [], [bOutT, bTfX], SO))
                    return steps
                ops.extend(mk())
            return ops
        if b == 0:
            pending_final.extend(final_ops())
        else:
            for op_ in final_ops():
                op_()

        for q4_ in range(4):
            bA4[q4_].merge_from(bA)
        bC.merge_from(bGb[0], bGb[1], bGb[2], bGb[3], bGhT, bHt2, bDtT)
        bW0.merge_from(bw1s[0], bw1s[1], bw2s[0])
        bW1.merge_from(bw2s[1])
        bE.merge_from(*bPT)

    P.plan["pool"].append(("w", SO, SO.v))
    P.check()

    with ExitStack() as es:
        for nm in P.sems:
            P.sems[nm].h = es.enter_context(nc.semaphore(nm))
        block = es.enter_context(nc.Block())
        block.tensor(lambda EG: P.run("pe", EG))
        block.scalar(lambda EG: P.run("act", EG))
        block.vector(lambda EG: P.run("dve", EG))
        block.gpsimd(lambda EG: P.run("pool", EG))
        block.sync(lambda EG: P.run("sp", EG))
    return nc


def kernel(L0_attn_x, L0_attn_Wq, L0_attn_Wk, L0_attn_Wv, L0_attn_Wo,
           L0_ffn_centroids, L0_ffn_gamma, L0_ffn_beta, L0_ffn_W1, L0_ffn_b1,
           L0_ffn_W2, L0_ffn_b2):
    x = np.ascontiguousarray(np.asarray(L0_attn_x, dtype=np.float32))
    Wq = np.ascontiguousarray(np.asarray(L0_attn_Wq, dtype=np.float32))
    Wk = np.ascontiguousarray(np.asarray(L0_attn_Wk, dtype=np.float32))
    Wv = np.ascontiguousarray(np.asarray(L0_attn_Wv, dtype=np.float32))
    Wo = np.ascontiguousarray(np.asarray(L0_attn_Wo, dtype=np.float32))
    cen = np.asarray(L0_ffn_centroids, dtype=np.float32)
    gam = np.asarray(L0_ffn_gamma, dtype=np.float32)
    bet = np.asarray(L0_ffn_beta, dtype=np.float32)
    W1 = np.asarray(L0_ffn_W1, dtype=np.float32)
    b1 = np.asarray(L0_ffn_b1, dtype=np.float32)
    W2 = np.asarray(L0_ffn_W2, dtype=np.float32)
    b2 = np.asarray(L0_ffn_b2, dtype=np.float32)

    w1h = np.ascontiguousarray(
        W1.reshape(E, 16, 256, 8, 128).transpose(0, 1, 4, 3, 2)).reshape(E, 16, 128, 2048)
    w2h = np.ascontiguousarray(
        W2.reshape(E, 8, 128, 32, 128).transpose(0, 1, 4, 3, 2)).reshape(E, 8, 128, 4096)
    b1h = np.ascontiguousarray(b1.reshape(E, 32, 128).transpose(2, 0, 1)).reshape(128, E * 32)
    b2h = np.ascontiguousarray(b2.reshape(E, 8, 128).transpose(2, 0, 1)).reshape(128, E * 8)
    gamh = np.ascontiguousarray(gam.reshape(E, 8, 128).transpose(2, 0, 1)).reshape(128, E * 8)
    beth = np.ascontiguousarray(bet.reshape(E, 8, 128).transpose(2, 0, 1)).reshape(128, E * 8)
    cenh = np.ascontiguousarray(cen.T.reshape(8, 128, 8).transpose(1, 0, 2)).reshape(128, 64)

    idfc = np.eye(128, dtype=np.float32)
    idbc = np.eye(128, dtype=np.float32).astype(ml_dtypes.bfloat16)
    mkbc = (np.arange(896)[None, :] >= (np.arange(128)[:, None] + 384)).astype(
        ml_dtypes.bfloat16)
    onec = np.ones((1, 64), dtype=np.float32)
    ltc = (np.arange(128)[:, None] < np.arange(128)[None, :]).astype(np.float32)
    o1rc = np.ones((1, 128), dtype=np.float32)
    o1cc = np.ones((128, 1), dtype=np.float32)
    cvecc = (np.arange(8, dtype=np.float32) * CAP).reshape(1, 8)
    p2tic = np.full((NSLOT, 128), float(T), dtype=np.float32)
    tokfc = (np.arange(NT, dtype=np.float32)[None, :] * 128
             + np.arange(128, dtype=np.float32)[:, None])

    nc = build_nc()
    shared = dict(wq=Wq, wk=Wk, wv=Wv, wo=Wo, w1=w1h, w2=w2h, b1=b1h, b2=b2h,
                  gam=gamh, bet=beth, cen=cenh, idf=idfc, idb=idbc, mkb=mkbc,
                  ones=onec, lt=ltc, o1r=o1rc, o1c=o1cc, cvec=cvecc,
                  p2ti=p2tic, tokf=tokfc)
    in_maps = [dict(shared, x=np.ascontiguousarray(x[2 * c:2 * c + 2]))
               for c in range(8)]
    res = run_bass_kernel_spmd(nc, in_maps, core_ids=list(range(8)))
    return np.concatenate([r["out"] for r in res.results], axis=0)


# version 18
# speedup vs baseline: 1.0072x; 1.0072x over previous
import numpy as np
import ml_dtypes
from contextlib import ExitStack

import concourse.bass as bass
import concourse.mybir as mybir
from concourse.bass_utils import run_bass_kernel_spmd
from concourse.bass_core_rust import IndirectOffsetOnAxis

F32 = mybir.dt.float32
BF16 = mybir.dt.bfloat16
I32 = mybir.dt.int32
AF = mybir.ActivationFunctionType
ALU = mybir.AluOpType
AX = mybir.AxisListType

B_CORE = 2
T = 2048
D = 1024
H = 16
E = 8
NT = 16
EPS = 1e-5
VS = 80
CAP = 384
NG = CAP // 128
NSLOT = 48


class Sem:
    def __init__(self, name, step):
        self.name, self.step, self.v, self.h = name, step, 0, None


class Buf:
    __slots__ = ("name", "w", "r")

    def __init__(self, name):
        self.name, self.w, self.r = name, {}, {}

    def merge_from(self, *others):
        for o in others:
            for s, c in o.w.items():
                self.w[s] = max(self.w.get(s, 0), c)
            for s, c in o.r.items():
                self.r[s] = max(self.r.get(s, 0), c)


class Prog:
    def __init__(self):
        self.plan = {e: [] for e in ("pe", "act", "dve", "pool", "sp")}
        self.sems = {nm: Sem(nm, step) for nm, step in
                     (("spe", 1), ("sact", 1), ("sdve", 1), ("spool", 1),
                      ("sx0", 16), ("sx1", 16), ("so", 16), ("swc", 16),
                      ("swh0", 16), ("swh1", 16), ("sw10", 16), ("sw11", 16),
                      ("sw20", 16), ("sw21", 16), ("sp2t", 16), ("sln0", 16),
                      ("sln1", 16), ("sdel", 16), ("sidx", 16), ("spm", 16),
                      ("sg0", 16), ("sg1", 16), ("sg2", 16), ("sg3", 16),
                      ("sds", 16), ("sdl0", 16), ("sdl1", 16),
                      ("ssd0", 16), ("ssd1", 16), ("ssd2", 16), ("ssd3", 16))}
        self.engsem = {"pe": self.sems["spe"], "act": self.sems["sact"],
                       "dve": self.sems["sdve"], "pool": self.sems["spool"]}
        self.waited = {e: {} for e in self.plan}

    def wait(self, eng, sem, val):
        if val <= 0:
            return
        hw = self.waited[eng]
        if hw.get(sem, 0) >= val:
            return
        hw[sem] = val
        self.plan[eng].append(("w", sem, val))

    def emit(self, eng, fn, reads=(), writes=(), sem=None):
        if sem is None:
            sem = self.engsem[eng]
        need = {}
        for b in tuple(reads) + tuple(writes):
            for s, c in b.w.items():
                if s is not sem:
                    need[s] = max(need.get(s, 0), c)
        for b in writes:
            for s, c in b.r.items():
                if s is not sem:
                    need[s] = max(need.get(s, 0), c)
        for s, c in need.items():
            self.wait(eng, s, c)
        sem.v += sem.step
        self.plan[eng].append(("i", fn, sem, sem.step))
        for b in reads:
            b.r[sem] = sem.v
        for b in writes:
            b.w[sem] = sem.v
            b.r.pop(sem, None)
        return sem.v

    def check(self):
        for eng, hw in self.waited.items():
            for sem, val in hw.items():
                assert val <= sem.v, (eng, sem.name, val, sem.v)

    def run(self, eng, EG):
        for item in self.plan[eng]:
            if item[0] == "w":
                EG.wait_ge(item[1].h, item[2])
            else:
                item[1](EG).then_inc(item[2].h, item[3])


def build_nc(debug=False):
    nc = bass.Bass("TRN2", target_bir_lowering=False)

    x_d = nc.dram_tensor("x", [B_CORE, T, D], F32, kind="ExternalInput").ap()
    wq_d = nc.dram_tensor("wq", [D, D], F32, kind="ExternalInput").ap()
    wk_d = nc.dram_tensor("wk", [D, D], F32, kind="ExternalInput").ap()
    wv_d = nc.dram_tensor("wv", [D, D], F32, kind="ExternalInput").ap()
    wo_d = nc.dram_tensor("wo", [D, D], F32, kind="ExternalInput").ap()
    w1_d = nc.dram_tensor("w1", [E, 16, 128, 2048], F32, kind="ExternalInput").ap()
    w2_d = nc.dram_tensor("w2", [E, 8, 128, 4096], F32, kind="ExternalInput").ap()
    b1_d = nc.dram_tensor("b1", [128, E * 32], F32, kind="ExternalInput").ap()
    b2_d = nc.dram_tensor("b2", [128, E * 8], F32, kind="ExternalInput").ap()
    gam_d = nc.dram_tensor("gam", [128, E * 8], F32, kind="ExternalInput").ap()
    bet_d = nc.dram_tensor("bet", [128, E * 8], F32, kind="ExternalInput").ap()
    cen_d = nc.dram_tensor("cen", [128, 64], F32, kind="ExternalInput").ap()
    idf_d = nc.dram_tensor("idf", [128, 128], F32, kind="ExternalInput").ap()
    idb_d = nc.dram_tensor("idb", [128, 128], BF16, kind="ExternalInput").ap()
    mkb_d = nc.dram_tensor("mkb", [128, 896], BF16, kind="ExternalInput").ap()
    one_d = nc.dram_tensor("ones", [1, 64], F32, kind="ExternalInput").ap()
    lt_d = nc.dram_tensor("lt", [128, 128], F32, kind="ExternalInput").ap()
    o1r_d = nc.dram_tensor("o1r", [1, 128], F32, kind="ExternalInput").ap()
    o1c_d = nc.dram_tensor("o1c", [128, 1], F32, kind="ExternalInput").ap()
    cvec_d = nc.dram_tensor("cvec", [1, 8], F32, kind="ExternalInput").ap()
    p2ti_d = nc.dram_tensor("p2ti", [NSLOT, 128], F32, kind="ExternalInput").ap()
    tokf_d = nc.dram_tensor("tokf", [128, NT], F32, kind="ExternalInput").ap()
    out_d = nc.dram_tensor("out", [B_CORE, T, D], F32, kind="ExternalOutput").ap()
    ln2_dram = nc.dram_tensor("ln2sc", [T + 1, D], BF16).ap()
    del_dram = nc.dram_tensor("delsc", [T + 1, D], BF16).ap()
    p2t_dram = nc.dram_tensor("p2tsc", [NSLOT, 128], F32).ap()
    s_dram = nc.dram_tensor("ssc", [T, D], BF16).ap()
    dbg_t = {}

    bufA = nc.alloc_sbuf_tensor("bufA", [128, 16384], BF16)
    bufB = nc.alloc_sbuf_tensor("bufB", [128, 16384], BF16)
    bufC = nc.alloc_sbuf_tensor("bufC", [128, 16 * 16 * 80], BF16)
    bufD = nc.alloc_sbuf_tensor("bufD", [128, 16384], BF16)
    bufE = nc.alloc_sbuf_tensor("bufE", [128, 4096], BF16)
    wbuf = nc.alloc_sbuf_tensor("wbuf", [128, 16384], BF16)
    mkb = nc.alloc_sbuf_tensor("mkb_s", [128, 896], BF16)
    idf = nc.alloc_sbuf_tensor("idf_s", [128, 128], F32)
    idb = nc.alloc_sbuf_tensor("idb_s", [128, 128], BF16)
    onesc = nc.alloc_sbuf_tensor("ones_s", [1, 64], F32)
    ltsb = nc.alloc_sbuf_tensor("lt_s", [128, 128], F32)
    o1r = nc.alloc_sbuf_tensor("o1r_s", [1, 128], F32)
    o1c = nc.alloc_sbuf_tensor("o1c_s", [128, 1], F32)
    cvecs = nc.alloc_sbuf_tensor("cvec_s", [1, 8], F32)
    cumrow = nc.alloc_sbuf_tensor("cum_s", [1, 8], F32)
    ctmp = nc.alloc_sbuf_tensor("ctmp_s", [1, 8], F32)
    rtsb = nc.alloc_sbuf_tensor("rt_s", [128, 8], F32)
    selr = nc.alloc_sbuf_tensor("selr_s", [128, 8], F32)
    dest_f = nc.alloc_sbuf_tensor("destf", [128, NT], F32)
    dest_i = nc.alloc_sbuf_tensor("desti", [128, NT], I32)
    tokf_sb = nc.alloc_sbuf_tensor("tokf_s", [128, NT], F32)
    pmcm = nc.alloc_sbuf_tensor("pmcm", [NSLOT, 128], F32)
    posm_f = nc.alloc_sbuf_tensor("posmf", [128, NSLOT], F32)
    posm_i = nc.alloc_sbuf_tensor("posmi", [128, NSLOT], I32)
    dtok2 = nc.alloc_sbuf_tensor("dtok2", [128, 1024], BF16)
    b1sb = nc.alloc_sbuf_tensor("b1_s", [128, E * 32], F32)
    b2sb = nc.alloc_sbuf_tensor("b2_s", [128, E * 8], F32)
    gamsb = nc.alloc_sbuf_tensor("gam_s", [128, E * 8], F32)
    betsb = nc.alloc_sbuf_tensor("bet_s", [128, E * 8], F32)
    cenT = nc.alloc_sbuf_tensor("cen_s", [128, 64], F32)
    xf = nc.alloc_sbuf_tensor("xf", [128, 2048], F32)
    tfA = nc.alloc_sbuf_tensor("tfA", [128, 1024], F32)
    tfA2 = nc.alloc_sbuf_tensor("tfA2", [128, 1024], F32)
    dtok = nc.alloc_sbuf_tensor("dtok", [128, 1024], BF16)
    scsb = nc.alloc_sbuf_tensor("scsb", [128, 16], F32)
    mks = nc.alloc_sbuf_tensor("mks", [128, 128], F32)
    lrep = nc.alloc_sbuf_tensor("lrep", [64, 1024], BF16)
    STN = ("s1", "s2", "mu1", "va", "rstd1", "rtmp", "std1", "s1b", "s2b",
           "mu2", "sq2", "c1", "c2", "tq", "mmax", "ea", "alpha")
    WIDE = ("mu1", "rstd1", "alpha", "c1", "c2")
    st = {k: nc.alloc_sbuf_tensor("st_" + k, [128, 2 * NT if k in WIDE else NT], F32)
          for k in STN}
    eps_sb = nc.alloc_sbuf_tensor("eps_s", [128, 1], F32)

    psSa = nc.alloc_psum_tensor("psSa", [128, 1024], F32)
    psSb = nc.alloc_psum_tensor("psSb", [128, 1024], F32)
    psE = nc.alloc_psum_tensor("psE", [128, 512], F32)
    psF = nc.alloc_psum_tensor("psF", [128, 512], F32)
    psT = nc.alloc_psum_tensor("psT", [128, 2048], BF16)

    P = Prog()
    SX = [P.sems["sx0"], P.sems["sx1"]]
    SO, SWC = P.sems["so"], P.sems["swc"]
    SWH = [P.sems["swh0"], P.sems["swh1"]]
    SW1 = [P.sems["sw10"], P.sems["sw11"]]
    SW2 = [P.sems["sw20"], P.sems["sw21"]]
    SP2T, SDEL, SIDX, SPM, SDS = (P.sems["sp2t"], P.sems["sdel"],
                                  P.sems["sidx"], P.sems["spm"], P.sems["sds"])
    SLN = [P.sems["sln0"], P.sems["sln1"]]
    SDL = [P.sems["sdl0"], P.sems["sdl1"]]
    SSD = [P.sems["ssd%d" % i] for i in range(4)]

    bA, bB, bC, bD, bE = Buf("A"), Buf("B"), Buf("C"), Buf("D"), Buf("E")
    bA4 = [Buf("A4_%d" % i) for i in range(4)]
    bCln, bCh = Buf("Cln"), Buf("Ch")
    bW0, bW1 = Buf("W0"), Buf("W1")
    bw1s = [Buf("w1s0"), Buf("w1s1")]
    bw2s = [Buf("w2s0"), Buf("w2s1")]
    bPT = [Buf("PT0"), Buf("PT1"), Buf("PT2"), Buf("PT3")]
    bXF = [Buf("xf0"), Buf("xf1")]
    bTfA, bTfA2, bDtok = Buf("tfA"), Buf("tfA2"), Buf("dtok")
    bSc = [Buf("sc0"), Buf("sc1")]
    bMk = Buf("mks")
    bCum, bRt, bSelr = Buf("cum"), Buf("rt"), Buf("selr")
    bCtmp = Buf("ctmp")
    bDest, bDestI = Buf("dest"), Buf("desti")
    bP2T, bLn2D, bDelD = Buf("p2t"), Buf("ln2d"), Buf("deld")
    bPmcm, bPosmF, bPosmI = Buf("pmcm"), Buf("posmf"), Buf("posmi")
    bSd = Buf("sdram")
    bGb = [Buf("gb0"), Buf("gb1"), Buf("gb2"), Buf("gb3")]
    bGhT, bHt2, bDtT, bZl = Buf("ghT"), Buf("ht2"), Buf("dtT"), Buf("zl")
    bDt2 = Buf("dtok2")
    bLrep = Buf("lrep")
    bST = {k: Buf("st_" + k) for k in STN}
    bConst = Buf("const")
    bSa0, bSa1 = Buf("psSa0"), Buf("psSa1")
    bSb0, bSb1 = Buf("psSb0"), Buf("psSb1")
    bPsE, bPsF = Buf("psE"), Buf("psF")
    bPsT = [Buf("psT0"), Buf("psT1")]
    bOutT = Buf("outT")

    mm1rot = [(psSa, 0, bSa0), (psSa, 512, bSa1), (psSb, 0, bSb0), (psSb, 512, bSb1)]

    def dma(eng, dst_ap, src_ap, wbufs, rbufs, sem):
        P.emit(eng, lambda EG, d=dst_ap, s=src_ap: EG.dma_start(out=d, in_=s),
               reads=rbufs, writes=wbufs, sem=sem)

    def dump(name, ap, shape, dtype, rbufs):
        if not debug:
            return
        dd = nc.dram_tensor(name, list(shape), dtype, kind="ExternalOutput").ap()
        dbg_t[name] = dd
        dma("sp", dd, ap, [], rbufs, P.sems["so"])

    for ci, (dst_t, src) in enumerate(
            ((mkb, mkb_d), (idf, idf_d), (idb, idb_d),
             (onesc, one_d), (b1sb, b1_d), (b2sb, b2_d),
             (gamsb, gam_d), (betsb, bet_d), (cenT, cen_d),
             (ltsb, lt_d), (o1r, o1r_d), (o1c, o1c_d), (cvecs, cvec_d),
             (tokf_sb, tokf_d))):
        P.wait("pool", SWC, 16 * ci)
        dma("pool", dst_t[:, :], src, [bConst], [], SWC)

    P.emit("dve", lambda EG: EG.memset(eps_sb[:, :], EPS), reads=[], writes=[bConst])

    def mm_group(out_ap, pairs, reads, writes):
        def fn(EG, o=out_ap, ps=tuple(pairs)):
            last = None
            n = len(ps)
            for i, (l, r) in enumerate(ps):
                last = EG.matmul(o, l, r, start=(i == 0), stop=(i == n - 1))
            return last
        return P.emit("pe", fn, reads=reads, writes=writes)

    def phase0_closures(b):
        ops = []
        def _pre(b=b):
            dma("pool", p2t_dram, p2ti_d, [bP2T], [], SP2T)
            P.emit("dve", lambda EG: EG.memset(cumrow[0:1, :], 0.0),
                   reads=[], writes=[bCum])
        ops.append(_pre)
        for t in range(NT):
            def _tile(b=b, t=t):
                sl = t % 2
                xfs = xf[:, sl * 1024:sl * 1024 + 1024]
                bxf = bXF[sl]
                if t == 0:
                    dma("sp", xfs, x_d[b, 0:128, :], [bxf], [], SX[0])
                if t + 1 < NT:
                    sl2 = (t + 1) % 2
                    dma("sp", xf[:, sl2 * 1024:sl2 * 1024 + 1024],
                        x_d[b, (t + 1) * 128:(t + 2) * 128, :], [bXF[sl2]], [], SX[sl2])
                P.emit("dve", lambda EG, i=xfs, o=st["s1"][:, t:t + 1]:
                       EG.reduce_sum(o, i, axis=AX.X), reads=[bxf], writes=[bST["s1"]])
                P.emit("act", lambda EG, i=xfs, a=st["s2"][:, t:t + 1]:
                       EG.activation(bufE[:, 0:1024], i, AF.Square, accum_out=a),
                       reads=[bxf], writes=[bPT[0], bST["s2"]])
                bc0 = b * NT
                P.emit("act", lambda EG, t=t, bc0=b * NT: EG.mul(st["mu1"][:, bc0 + t:bc0 + t + 1],
                                                     st["s1"][:, t:t + 1], 1.0 / D),
                       reads=[bST["s1"]], writes=[bST["mu1"]])
                P.emit("pool", lambda EG, t=t, bc0=b * NT: EG.tensor_tensor(
                    st["tq"][:, t:t + 1], st["mu1"][:, bc0 + t:bc0 + t + 1],
                    st["mu1"][:, bc0 + t:bc0 + t + 1], ALU.mult),
                    reads=[bST["mu1"]], writes=[bST["tq"]])
                P.emit("dve", lambda EG, t=t: EG.scalar_tensor_tensor(
                    st["va"][:, t:t + 1], st["s2"][:, t:t + 1], 1.0 / D,
                    st["tq"][:, t:t + 1], ALU.mult, ALU.subtract),
                    reads=[bST["s2"], bST["tq"]], writes=[bST["va"]])
                P.emit("act", lambda EG, t=t: EG.activation(
                    st["std1"][:, t:t + 1], st["va"][:, t:t + 1], AF.Sqrt,
                    bias=eps_sb[:, 0:1]),
                    reads=[bST["va"], bConst], writes=[bST["std1"]])
                P.emit("dve", lambda EG, t=t: EG.reciprocal(
                    st["rtmp"][:, t:t + 1], st["std1"][:, t:t + 1]),
                    reads=[bST["std1"]], writes=[bST["rtmp"]])
                P.emit("pool", lambda EG, t=t, bc0=b * NT: EG.tensor_copy(
                    st["rstd1"][:, bc0 + t:bc0 + t + 1], st["rtmp"][:, t:t + 1]),
                    reads=[bST["rtmp"]], writes=[bST["rstd1"]])
                tfX = tfA if t % 2 == 0 else tfA2
                bTfX = bTfA if t % 2 == 0 else bTfA2
                P.emit("dve", lambda EG, i=xfs, t=t, o=tfX, bc0=b * NT:
                       EG.tensor_scalar(o[:, :], i, st["mu1"][:, bc0 + t:bc0 + t + 1],
                                        st["rstd1"][:, bc0 + t:bc0 + t + 1],
                                        ALU.subtract, ALU.mult),
                       reads=[bxf, bST["mu1"], bST["rstd1"]], writes=[bTfX])
                dslot0 = bufE[:, (1 + t % 2) * 1024:(2 + t % 2) * 1024]
                bds0 = bPT[1 + t % 2]
                P.emit("act", lambda EG, i=tfX, o=dslot0: EG.copy(o, i[:, :]),
                       reads=[bTfX], writes=[bds0])
                P.emit("dve", lambda EG, o=st["s1b"][:, t:t + 1], i=tfX:
                       EG.reduce_sum(o, i[:, :], axis=AX.X),
                       reads=[bTfX], writes=[bST["s1b"]])
                P.emit("act", lambda EG, a=st["s2b"][:, t:t + 1], i=tfX:
                       EG.activation(bufE[:, 0:1024], i[:, :], AF.Square, accum_out=a),
                       reads=[bTfX], writes=[bPT[0], bST["s2b"]])
                P.emit("act", lambda EG, t=t: EG.mul(st["mu2"][:, t:t + 1],
                                                     st["s1b"][:, t:t + 1], 1.0 / D),
                       reads=[bST["s1b"]], writes=[bST["mu2"]])
                P.emit("pool", lambda EG, t=t: EG.tensor_tensor(
                    st["tq"][:, t:t + 1], st["mu2"][:, t:t + 1],
                    st["mu2"][:, t:t + 1], ALU.mult),
                    reads=[bST["mu2"]], writes=[bST["tq"]])
                P.emit("dve", lambda EG, t=t: EG.scalar_tensor_tensor(
                    st["va"][:, t:t + 1], st["s2b"][:, t:t + 1], 1.0 / D,
                    st["tq"][:, t:t + 1], ALU.mult, ALU.subtract),
                    reads=[bST["s2b"], bST["tq"]], writes=[bST["va"]])
                P.emit("act", lambda EG, t=t: EG.activation(
                    st["sq2"][:, t:t + 1], st["va"][:, t:t + 1], AF.Sqrt,
                    bias=eps_sb[:, 0:1]),
                    reads=[bST["va"], bConst], writes=[bST["sq2"]])
                P.emit("pool", lambda EG, t=t: EG.tensor_tensor(
                    st["ea"][:, t:t + 1], st["std1"][:, t:t + 1],
                    st["sq2"][:, t:t + 1], ALU.mult),
                    reads=[bST["std1"], bST["sq2"]], writes=[bST["ea"]])
                P.emit("dve", lambda EG, t=t, bc0=b * NT: EG.reciprocal(
                    st["c2"][:, bc0 + t:bc0 + t + 1], st["ea"][:, t:t + 1]),
                    reads=[bST["ea"]], writes=[bST["c2"]])
                P.emit("dve", lambda EG, t=t, bc0=b * NT: EG.scalar_tensor_tensor(
                    st["c1"][:, bc0 + t:bc0 + t + 1], st["mu2"][:, t:t + 1],
                    st["std1"][:, t:t + 1],
                    st["mu1"][:, bc0 + t:bc0 + t + 1], ALU.mult, ALU.add),
                    reads=[bST["mu2"], bST["std1"], bST["mu1"]], writes=[bST["c1"]])
                half = t % 2
                pshalf = psT[:, half * 1024:half * 1024 + 1024]
                def _trb(EG, ph=pshalf, i=dslot0):
                    last = None
                    for c in range(8):
                        last = EG.transpose(ph[:, c * 128:(c + 1) * 128],
                                            i[:, c * 128:(c + 1) * 128], idb[:, :])
                    return last
                P.emit("pe", _trb, reads=[bds0, bConst], writes=[bPsT[half]])
                dstA = bufA[:, 0:8 * T].rearrange("p (c q) -> p c q", c=8)[:, :, t * 128:(t + 1) * 128]
                srcT = pshalf.rearrange("p (c q) -> p c q", c=8)
                P.emit("dve", lambda EG, d2=dstA, s2=srcT: EG.tensor_copy(d2, s2),
                       reads=[bPsT[half]], writes=[bA4[t // 4]])
                for c in range(8):
                    ps, pb = (psE, bPsE) if c % 2 == 0 else (psF, bPsF)
                    P.emit("pe", lambda EG, c=c, ps=ps, i=tfX:
                           EG.transpose(ps[:, 0:128], i[:, c * 128:(c + 1) * 128], idf[:, :]),
                           reads=[bTfX, bConst], writes=[pb])
                    P.emit("act", lambda EG, c=c, ps=ps, o=xfs:
                           EG.copy(o[:, c * 128:(c + 1) * 128], ps[:, 0:128]),
                           reads=[pb], writes=[bxf])
                mm_group(psSa[:, 0:8],
                         [(xfs[:, c * 128:(c + 1) * 128], cenT[:, c * 8:(c + 1) * 8])
                          for c in range(8)],
                         reads=[bxf, bConst], writes=[bSa0])
                ssl = scsb[:, sl * 8:sl * 8 + 8]
                msl = mks[:, t * 8:t * 8 + 8]
                P.emit("act", lambda EG, o=ssl: EG.copy(o, psSa[:, 0:8]),
                       reads=[bSa0], writes=[bSc[sl]])
                P.emit("dve", lambda EG, i=ssl, t=t:
                       EG.reduce_max(st["mmax"][:, t:t + 1], i, axis=AX.X),
                       reads=[bSc[sl]], writes=[bST["mmax"]])
                P.emit("pool", lambda EG, i=ssl, o=msl, t=t:
                       EG.tensor_single_scalar(o, i, st["mmax"][:, t:t + 1], ALU.is_equal),
                       reads=[bSc[sl], bST["mmax"]], writes=[bMk])
                def _rank(EG, m=msl):
                    EG.matmul(psSb[:, 0:8], ltsb[:, :], m, start=True, stop=False)
                    EG.matmul(psSb[:, 0:8], o1r[0:1, :], cumrow[0:1, :], start=False, stop=False)
                    return EG.matmul(psSb[:, 0:8], o1r[0:1, :], cvecs[0:1, :], start=False, stop=True)
                P.emit("pe", _rank, reads=[bConst, bMk, bCum], writes=[bSb0])
                P.emit("pe", lambda EG, m=msl: EG.matmul(
                    psSb[0:1, 512:520], o1c[:, :], m, start=True, stop=True),
                    reads=[bConst, bMk], writes=[bSb1])
                P.emit("act", lambda EG: EG.copy(ctmp[0:1, :], psSb[0:1, 512:520]),
                       reads=[bSb1], writes=[bCtmp])
                P.emit("pool", lambda EG: EG.tensor_tensor(
                    cumrow[0:1, :], cumrow[0:1, :], ctmp[0:1, :], ALU.add),
                    reads=[bCtmp, bCum], writes=[bCum])
                P.emit("act", lambda EG: EG.copy(rtsb[:, :], psSb[:, 0:8]),
                       reads=[bSb0], writes=[bRt])
                P.emit("pool", lambda EG, m=msl: EG.tensor_tensor(selr[:, :], m, rtsb[:, :], ALU.mult),
                       reads=[bMk, bRt], writes=[bSelr])
                P.emit("dve", lambda EG, t=t: EG.reduce_sum(
                    dest_f[:, t:t + 1], selr[:, :], axis=AX.X),
                    reads=[bSelr], writes=[bDest])

            ops.append(_tile)
        def _post(b=b):
            P.emit("act", lambda EG: EG.activation(st["ea"][:, :], st["mmax"][:, :],
                                                   AF.Exp, scale=-1.0),
                   reads=[bST["mmax"]], writes=[bST["ea"]])
            P.emit("pool", lambda EG: EG.tensor_scalar(
                st["tq"][:, :], st["ea"][:, :], 1.0, None, ALU.add),
                reads=[bST["ea"]], writes=[bST["tq"]])
            P.emit("dve", lambda EG, bc0=b * NT: EG.reciprocal(
                st["alpha"][:, bc0:bc0 + NT], st["tq"][:, :]),
                reads=[bST["tq"]], writes=[bST["alpha"]])
            P.emit("dve", lambda EG: EG.tensor_copy(dest_i[:, :], dest_f[:, :]),
                   reads=[bDest], writes=[bDestI])
            p2t_flat = p2t_dram.rearrange("a (b c) -> (a b) c", c=1)
            for t in range(NT):
                P.emit("pool", lambda EG, t=t: EG.indirect_dma_start(
                    out=p2t_flat,
                    out_offset=IndirectOffsetOnAxis(ap=dest_i[:, t:t + 1], axis=0),
                    in_=tokf_sb[:, t:t + 1], in_offset=None),
                    reads=[bDestI, bConst, bP2T], writes=[bP2T], sem=SIDX)
            dma("pool", pmcm[:, :], p2t_dram, [bPmcm], [bP2T], SPM)
            P.emit("pe", lambda EG: EG.transpose(psE[:, 0:NSLOT], pmcm[:, :],
                                                 idf[0:NSLOT, 0:NSLOT]),
                   reads=[bPmcm, bConst], writes=[bPsE])
            P.emit("act", lambda EG: EG.copy(posm_f[:, :], psE[:, 0:NSLOT]),
                   reads=[bPsE], writes=[bPosmF])
            P.emit("dve", lambda EG: EG.tensor_copy(posm_i[:, :], posm_f[:, :]),
                   reads=[bPosmF], writes=[bPosmI])

            if debug and b == 0:
                for nm in ("mu1", "rstd1", "c1", "c2", "alpha", "mmax",
                           "s1", "s2", "va", "tq", "std1", "s1b", "s2b",
                           "mu2", "sq2", "ea"):
                    dump("d_" + nm, st[nm][:, 0:NT], (128, NT), F32, [bST[nm]])
                dump("d_xnT", bufA[:, :], (128, 16384), BF16, [bA])
                dump("d_dest", dest_f[:, :], (128, NT), F32, [bDest])
                dump("d_posm", posm_f[:, :], (128, NSLOT), F32, [bPosmF])

        ops.append(_post)
        return ops

    pending_final = []
    pending_p0 = []
    pending_p0_post = []
    for b in range(B_CORE):
        if b == 0:
            for _c in phase0_closures(0):
                _c()
        while pending_p0:
            pending_p0.pop(0)()
        while pending_p0_post:
            pending_p0_post.pop(0)()
        def load_w(dram, half, bhalf):
            for k in range(8):
                P.wait("pool", SWH[half], SWH[half].v)
                dma("pool",
                    wbuf[:, half * 8192 + k * 1024: half * 8192 + (k + 1) * 1024],
                    dram[k * 128:(k + 1) * 128, :], [bhalf], [], SWH[half])
        load_w(wq_d, 0, bW0)
        load_w(wk_d, 1, bW1)

        for (wsl, bws, dstbuf, bdst) in ((0, bW0, bufD, bD), (1, bW1, bufB, bB)):
            for c in range(8):
                for q4 in range(4):
                    pst, off, bbank = mm1rot[(c * 4 + q4) % 4]
                    out_ap = pst[:, off:off + 512]
                    pairs = [(wbuf[:, wsl * 8192 + k * 1024 + c * 128:
                                   wsl * 8192 + k * 1024 + c * 128 + 128],
                              bufA[:, k * T + q4 * 512: k * T + q4 * 512 + 512])
                             for k in range(8)]
                    mm_group(out_ap, pairs, reads=[bws, bA4[q4]], writes=[bbank])
                    P.emit("dve", lambda EG,
                           o=dstbuf[:, c * T + q4 * 512: c * T + q4 * 512 + 512],
                           i=out_ap: EG.tensor_copy(o, i),
                           reads=[bbank], writes=[bdst])
                    if pending_final:
                        pending_final.pop(0)()
                    if pending_final:
                        pending_final.pop(0)()
        ones_ap = bufC[:, 0:16 * 16 * VS].rearrange(
            "p (kt h j) -> p kt h j", h=H, j=VS)[:, :, :, 64:65]
        P.emit("dve", lambda EG, a=ones_ap: EG.memset(a, 1.0),
               reads=[], writes=[bC])
        load_w(wv_d, 0, bW0)
        for kt in range(16):
            for hf in range(2):
                pst, off, bbank = mm1rot[(kt * 2 + hf) % 4]
                out_ap = pst[:, off:off + 512]
                pairs = [(bufA[:, k * T + kt * 128: k * T + (kt + 1) * 128],
                          wbuf[:, k * 1024 + hf * 512: k * 1024 + hf * 512 + 512])
                         for k in range(8)]
                mm_group(out_ap, pairs, reads=[bA4[kt // 4], bW0], writes=[bbank])
                if pending_final:
                    pending_final.pop(0)()
                if pending_final:
                    pending_final.pop(0)()
                dstV = bufC[:, kt * 1280 + hf * 8 * VS: kt * 1280 + (hf + 1) * 8 * VS]\
                    .rearrange("p (h j) -> p h j", j=VS)[:, :, 0:64]
                srcV = out_ap.rearrange("p (h j) -> p h j", j=64)
                P.emit("dve", lambda EG, d2=dstV, s2=srcV: EG.tensor_copy(d2, s2),
                       reads=[bbank], writes=[bC])

        if debug and b == 0:
            dump("d_qt", bufD[:, :], (128, 16384), BF16, [bD])
            dump("d_kt", bufB[:, :], (128, 16384), BF16, [bB])
            dump("d_vp", bufC[:, :], (128, 16 * 16 * 80), BF16, [bC])

        for _pt in bPT:
            _pt.merge_from(bE)
        bA.merge_from(*bA4)
        load_w(wo_d, 1, bW1)

        while pending_final:
            pending_final.pop(0)()

        pending_tail = [None]

        def emit_tail(hp, qg):
            def tail():
                def _ln(EG):
                    EG.activation(tfA[32:33, 0:512], psE[64:65, :], AF.Ln)
                    return EG.activation(tfA[32:33, 512:1024], psF[64:65, :], AF.Ln)
                P.emit("act", _ln, reads=[bPsE, bPsF], writes=[bTfA])
                P.emit("act", lambda EG: EG.activation(
                    tfA[0:1, 0:1024], tfA[32:33, 0:1024], AF.Exp, scale=-1.0),
                    reads=[bTfA], writes=[bTfA])
                def _lbc(EG):
                    EG.matmul(psE[64:128, :], onesc[0:1, :], tfA[0:1, 0:512],
                              start=True, stop=True)
                    return EG.matmul(psF[64:128, :], onesc[0:1, :],
                                     tfA[0:1, 512:1024], start=True, stop=True)
                P.emit("pe", _lbc, reads=[bTfA, bConst], writes=[bPsE, bPsF])
                def _lrp(EG):
                    EG.copy(lrep[0:64, 0:512], psE[64:128, :])
                    return EG.copy(lrep[0:64, 512:1024], psF[64:128, :])
                P.emit("act", _lrp, reads=[bPsE, bPsF], writes=[bLrep])
                def _cev(EG, hp=hp, qg=qg):
                    EG.tensor_tensor(
                        bufA[0:64, hp * T + qg * 512: hp * T + qg * 512 + 512],
                        psE[0:64, :], lrep[0:64, 0:512], ALU.mult)
                    return EG.tensor_tensor(
                        bufA[64:128, hp * T + qg * 512: hp * T + qg * 512 + 512],
                        psF[0:64, :], lrep[0:64, 512:1024], ALU.mult)
                P.emit("dve", _cev, reads=[bPsE, bPsF, bLrep], writes=[bA])
            return tail

        for hp in range(8):
            for qg in (3, 2, 1, 0):
                ktmax = 4 * qg + 4

                def emit_S(kt, hp=hp, qg=qg):
                    pair = psSb if kt % 2 == 0 else psSa
                    bp = (bSb0, bSb1) if kt % 2 == 0 else (bSa0, bSa1)
                    def _sp(EG, pp=pair, hp=hp, qg=qg, kt=kt):
                        EG.matmul(pp[:, 0:512],
                                  bufB[0:64, hp * T + kt * 128: hp * T + (kt + 1) * 128],
                                  bufD[0:64, hp * T + qg * 512: hp * T + qg * 512 + 512],
                                  start=True, stop=True, tile_position=(0, 0))
                        return EG.matmul(
                            pp[:, 512:1024],
                            bufB[64:128, hp * T + kt * 128: hp * T + (kt + 1) * 128],
                            bufD[64:128, hp * T + qg * 512: hp * T + qg * 512 + 512],
                            start=True, stop=True, tile_position=(64, 0))
                    P.emit("pe", _sp, reads=[bB, bD], writes=list(bp))
                    sl = kt % 4
                    P.emit("act", lambda EG, pp=pair, sl=sl:
                           EG.activation(bufE[:, sl * 1024: sl * 1024 + 1024],
                                         pp[:, 0:1024], AF.Exp, scale=0.125),
                           reads=list(bp), writes=[bPT[sl]])
                    j = kt - 4 * qg
                    if j >= 0:
                        def _mask(EG, sl=sl, j=j):
                            EG.tensor_tensor(bufE[:, sl * 1024: sl * 1024 + 512],
                                             bufE[:, sl * 1024: sl * 1024 + 512],
                                             mkb[:, 384 - 128 * j: 896 - 128 * j], ALU.mult)
                            return EG.tensor_tensor(
                                bufE[:, sl * 1024 + 512: sl * 1024 + 1024],
                                bufE[:, sl * 1024 + 512: sl * 1024 + 1024],
                                mkb[:, 384 - 128 * j: 896 - 128 * j], ALU.mult)
                        P.emit("dve", _mask, reads=[bConst], writes=[bPT[sl]])

                def emit_ctx(kt, hp=hp, ktmax=ktmax):
                    sl = kt % 4
                    def _ctx(EG, hp=hp, kt=kt, sl=sl,
                             first=(kt == 0), last=(kt == ktmax - 1)):
                        EG.matmul(psE[0:65, :],
                                  bufC[:, kt * 1280 + (2 * hp) * VS:
                                       kt * 1280 + (2 * hp) * VS + 65],
                                  bufE[:, sl * 1024: sl * 1024 + 512],
                                  start=first, stop=last)
                        return EG.matmul(psF[0:65, :],
                                         bufC[:, kt * 1280 + (2 * hp + 1) * VS:
                                              kt * 1280 + (2 * hp + 1) * VS + 65],
                                         bufE[:, sl * 1024 + 512: sl * 1024 + 1024],
                                         start=first, stop=last)
                    P.emit("pe", _ctx, reads=[bC, bPT[sl]], writes=[bPsE, bPsF])

                for kt in range(ktmax):
                    emit_S(kt)
                    if kt == 2 and pending_tail[0] is not None:
                        pending_tail[0]()
                        pending_tail[0] = None
                    if kt >= 3:
                        emit_ctx(kt - 3)
                if pending_tail[0] is not None:
                    pending_tail[0]()
                    pending_tail[0] = None
                for kt in range(max(0, ktmax - 3), ktmax):
                    emit_ctx(kt)
                pending_tail[0] = emit_tail(hp, qg)
        pending_tail[0]()
        pending_tail[0] = None

        if debug and b == 0:
            dump("d_ctx", bufA[:, :], (128, 16384), BF16, [bA])

        for qt in range(16):
            for dh in range(2):
                pst, pb = (psE, bPsE) if (qt * 2 + dh) % 2 == 0 else (psF, bPsF)
                pairs = [(bufA[:, c * T + qt * 128: c * T + (qt + 1) * 128],
                          wbuf[:, 8192 + c * 1024 + dh * 512:
                               8192 + c * 1024 + dh * 512 + 512])
                         for c in range(8)]
                mm_group(pst[:, :], pairs, reads=[bA, bW1], writes=[pb])
                ssl_ = qt % 4
                sap = bufE[:, ssl_ * 1024 + dh * 512: ssl_ * 1024 + dh * 512 + 512]
                P.emit("dve", lambda EG, o=sap, i=pst[:, :]: EG.tensor_copy(o, i),
                       reads=[pb], writes=[bPT[ssl_]])
                dma("sp", s_dram[qt * 128:(qt + 1) * 128, dh * 512:dh * 512 + 512],
                    sap, [bSd], [bPT[ssl_]], SSD[ssl_])

        if debug and b == 0:
            dump("d_s", bufB[:, :], (128, 16384), BF16, [bB])

        bZl.merge_from(bC)
        bGb[0].merge_from(bC)
        bGb[1].merge_from(bC)
        bGhT.merge_from(bC)
        bHt2.merge_from(bC)
        bDtT.merge_from(bC)
        for s_ in bw1s + bw2s:
            s_.merge_from(bW0, bW1)

        P.emit("dve", lambda EG: EG.memset(bufC[:, 0:1024], 0.0),
               reads=[], writes=[bGb[0]])
        for t in range(NT):
            sl = t % 2
            xfs = xf[:, sl * 1024:sl * 1024 + 1024]
            bxf = bXF[sl]
            if t == 0:
                dma("sp", xfs, x_d[b, 0:128, :], [bxf], [], SX[0])
            if t + 1 < NT:
                sl2 = (t + 1) % 2
                dma("sp", xf[:, sl2 * 1024:sl2 * 1024 + 1024],
                    x_d[b, (t + 1) * 128:(t + 2) * 128, :], [bXF[sl2]], [], SX[sl2])
            dslot = dtok if sl == 0 else dtok2
            bdsl = bDtok if sl == 0 else bDt2
            P.emit("dve", lambda EG, i=xfs, t=t, o=dslot, bc0=b * NT:
                   EG.tensor_scalar(o[:, :], i, st["c1"][:, bc0 + t:bc0 + t + 1],
                                    st["c2"][:, bc0 + t:bc0 + t + 1],
                                    ALU.subtract, ALU.mult),
                   reads=[bxf, bST["c1"], bST["c2"]], writes=[bdsl])
            dma("sp", ln2_dram[t * 128:(t + 1) * 128, :], dslot[:, :],
                [bLn2D], [bdsl], SLN[sl])
            dma("sp", del_dram[t * 128:(t + 1) * 128, :], bufC[:, 0:1024],
                [bDelD], [bGb[0]], SDEL)

        w1q = [(e_, fs_) for e_ in range(E) for fs_ in range(16)]
        w2q = [(e_, dc_) for e_ in range(E) for dc_ in range(8)]
        wp = {"w1": 0, "w2": 0}
        def pf1():
            if wp["w1"] < len(w1q):
                e_, fs_ = w1q[wp["w1"]]
                wp["w1"] += 1
                sl_ = fs_ % 2
                dma("pool", wbuf[:, sl_ * 2048:(sl_ + 1) * 2048],
                    w1_d[e_, fs_], [bw1s[sl_]], [], SW1[sl_])
        def pf2():
            if wp["w2"] < len(w2q):
                e_, dc_ = w2q[wp["w2"]]
                wp["w2"] += 1
                sl_ = dc_ % 2
                dma("pool", wbuf[:, 4096 + sl_ * 4096: 4096 + (sl_ + 1) * 4096],
                    w2_d[e_, dc_], [bw2s[sl_]], [], SW2[sl_])
        pf1(); pf1(); pf2(); pf2()

        bGh = [bGhT, bHt2]
        dtT = bufC[:, 12288:12288 + 8 * CAP].rearrange("p (c q) -> p c q", c=8)

        defq = []
        def gather_emit(e):
            par = e % 2
            for g in range(NG):
                gb = bufC[:, g * 1024:(g + 1) * 1024]
                col = e * NG + g
                P.emit("pool", lambda EG, gb=gb, col=col: EG.indirect_dma_start(
                    out=gb, out_offset=None,
                    in_=ln2_dram,
                    in_offset=IndirectOffsetOnAxis(ap=posm_i[:, col:col + 1], axis=0)),
                    reads=[bPosmI, bLn2D], writes=[bGb[g]],
                    sem=P.sems["sg%d" % g])
        def defer_gather_tr(e):
            par = e % 2
            gbase = 4096 + par * 4096
            for g in range(NG):
                def emit_tr(g=g, gbase=gbase, par=par):
                    gb = bufC[:, g * 1024:(g + 1) * 1024]
                    half = g % 2
                    pshalf = psT[:, half * 1024:half * 1024 + 1024]
                    def _trg(EG, gb=gb, ph=pshalf):
                        last = None
                        for c in range(8):
                            last = EG.transpose(ph[:, c * 128:(c + 1) * 128],
                                                gb[:, c * 128:(c + 1) * 128], idb[:, :])
                        return last
                    P.emit("pe", _trg, reads=[bGb[g], bConst], writes=[bPsT[half]])
                    src_t = pshalf.rearrange("p (c q) -> p c q", c=8)
                    dst_t = bufC[:, gbase:gbase + 8 * CAP].rearrange(
                        "p (c q) -> p c q", c=8)[:, :, g * 128:(g + 1) * 128]
                    P.emit("dve", lambda EG, d2=dst_t, s2=src_t: EG.tensor_copy(d2, s2),
                           reads=[bPsT[half]], writes=[bGh[par]])
                defq.append(emit_tr)
            def emit_ht(e=e, gbase=gbase, par=par):
                def _ht(EG, e=e, gbase=gbase):
                    last = None
                    for c in range(8):
                        sl_ = bufC[:, gbase + c * CAP: gbase + (c + 1) * CAP]
                        last = EG.tensor_scalar(
                            sl_, sl_,
                            gamsb[:, e * 8 + c: e * 8 + c + 1],
                            betsb[:, e * 8 + c: e * 8 + c + 1],
                            ALU.mult, ALU.add)
                    return last
                P.emit("dve", _ht, reads=[bGh[par], bConst], writes=[bGh[par]])
            defq.append(emit_ht)
        def defer_back_tr(e):
            par = e % 2
            for g in range(NG):
                def emit_bt(g=g, e=e):
                    half = g % 2
                    pshalf = psT[:, half * 1024:half * 1024 + 1024]
                    def _trd(EG, ph=pshalf, g=g):
                        last = None
                        for c in range(8):
                            last = EG.transpose(ph[:, c * 128:(c + 1) * 128],
                                                dtT[:, c, g * 128:(g + 1) * 128], idb[:, :])
                        return last
                    P.emit("pe", _trd, reads=[bDtT, bConst], writes=[bPsT[half]])
                    dslot = dtok if g % 2 == 0 else dtok2
                    bdsl = bDtok if g % 2 == 0 else bDt2
                    P.emit("dve", lambda EG, ph=pshalf, o=dslot: EG.tensor_copy(o[:, :], ph),
                           reads=[bPsT[half]], writes=[bdsl])
                    col = e * NG + g
                    P.emit("pool", lambda EG, o=dslot, col=col: EG.indirect_dma_start(
                        out=del_dram, out_offset=IndirectOffsetOnAxis(
                            ap=posm_i[:, col:col + 1], axis=0),
                        in_=o[:, :], in_offset=None),
                        reads=[bPosmI, bdsl, bDelD], writes=[bDelD], sem=SDS)
                defq.append(emit_bt)

        if b == 0:
            _p0 = phase0_closures(1)
            pending_p0.extend(_p0[:-1])
            pending_p0_post.append(_p0[-1])
        gather_emit(0)
        defer_gather_tr(0)
        while defq:
            defq.pop(0)()
        for e in range(E):
            par = e % 2
            gbase = 4096 + par * 4096
            if e + 1 < E:
                gather_emit(e + 1)
                defer_gather_tr(e + 1)
            for fc in range(32):
                fs, fc2 = fc // 2, fc % 2
                wsl = fs % 2
                pst, off, bbank = mm1rot[fc % 4]
                out_ap = pst[:, off:off + CAP]
                pairs = [(wbuf[:, wsl * 2048 + dcc * 256 + fc2 * 128:
                               wsl * 2048 + dcc * 256 + fc2 * 128 + 128],
                          bufC[:, gbase + dcc * CAP: gbase + (dcc + 1) * CAP])
                         for dcc in range(8)]
                mm_group(out_ap, pairs, reads=[bw1s[wsl], bGh[par]], writes=[bbank])
                P.emit("act", lambda EG, o=bufD[:, fc * CAP:(fc + 1) * CAP], i=out_ap,
                       bc=b1sb[:, e * 32 + fc: e * 32 + fc + 1]:
                       EG.activation(o, i, AF.Relu, bias=bc),
                       reads=[bbank, bConst], writes=[bD])
                if fc % 2 == 1:
                    pf1()
                if fc % 3 == 2 and defq:
                    defq.pop(0)()
                if fc % 4 == 1 and pending_p0:
                    pending_p0.pop(0)()

            for dc in range(8):
                wsl = dc % 2
                pst, pb = (psE, bPsE) if dc % 2 == 0 else (psF, bPsF)
                pairs = [(wbuf[:, 4096 + wsl * 4096 + fc * 128:
                               4096 + wsl * 4096 + (fc + 1) * 128],
                          bufD[:, fc * CAP:(fc + 1) * CAP])
                         for fc in range(32)]
                mm_group(pst[:, 0:CAP], pairs, reads=[bw2s[wsl], bD], writes=[pb])
                P.emit("act", lambda EG, o=bufC[:, 12288 + dc * CAP: 12288 + (dc + 1) * CAP],
                       i=pst[:, 0:CAP], bc=b2sb[:, e * 8 + dc: e * 8 + dc + 1]:
                       EG.activation(o, i, AF.Identity, bias=bc),
                       reads=[pb, bConst], writes=[bDtT])
                pf2()

            defer_back_tr(e)
        while defq:
            defq.pop(0)()

        def final_ops(b=b):
            ops = []
            for t in range(NT):
                def mk(t=t, b=b):
                    sl = t % 2
                    xfs = xf[:, sl * 1024:sl * 1024 + 1024]
                    bxf = bXF[sl]
                    dslot = dtok if sl == 0 else dtok2
                    bdsl = bDtok if sl == 0 else bDt2
                    tfX = tfA if t % 2 == 0 else tfA2
                    bTfX = bTfA if t % 2 == 0 else bTfA2
                    ssl_ = t % 4
                    sap = bufE[:, ssl_ * 1024: ssl_ * 1024 + 1024]
                    steps = []
                    steps.append(lambda: dma("sp", xfs,
                                             x_d[b, t * 128:(t + 1) * 128, :],
                                             [bxf], [], SX[sl]))
                    steps.append(lambda: dma("sp", dslot[:, :],
                                             del_dram[t * 128:(t + 1) * 128, :],
                                             [bdsl], [bDelD], SDL[sl]))
                    steps.append(lambda: dma("sp", sap,
                                             s_dram[t * 128:(t + 1) * 128, :],
                                             [bPT[ssl_]], [bSd], SSD[ssl_]))
                    def _fc1(EG, i=xfs, t=t, ds=dslot, o=tfX, bc0=b * NT):
                        EG.tensor_scalar(o[:, :], i,
                                         st["mu1"][:, bc0 + t:bc0 + t + 1],
                                         st["rstd1"][:, bc0 + t:bc0 + t + 1],
                                         ALU.subtract, ALU.mult)
                        return EG.scalar_tensor_tensor(
                            o[:, :], ds[:, :],
                            st["alpha"][:, bc0 + t:bc0 + t + 1],
                            o[:, :], ALU.mult, ALU.add)
                    steps.append(lambda: P.emit(
                        "dve", _fc1,
                        reads=[bxf, bdsl, bST["mu1"], bST["rstd1"], bST["alpha"]],
                        writes=[bTfX]))
                    steps.append(lambda: P.emit(
                        "act", lambda EG, ds=dslot, i=tfX:
                        EG.activation(ds[:, :], i[:, :], AF.Silu),
                        reads=[bTfX], writes=[bdsl]))
                    def _fc2(EG, i=xfs, ds=dslot, o=tfX, sp2=sap):
                        EG.tensor_tensor(o[:, :], ds[:, :], sp2, ALU.mult)
                        return EG.tensor_tensor(o[:, :], i, o[:, :], ALU.add)
                    steps.append(lambda: P.emit(
                        "dve", _fc2, reads=[bxf, bdsl, bPT[ssl_]],
                        writes=[bTfX, bOutT]))
                    steps.append(lambda: dma("sp", out_d[b, t * 128:(t + 1) * 128, :],
                                             tfX[:, :], [], [bOutT, bTfX], SO))
                    return steps
                ops.extend(mk())
            return ops
        if b == 0:
            pending_final.extend(final_ops())
        else:
            for op_ in final_ops():
                op_()

        for q4_ in range(4):
            bA4[q4_].merge_from(bA)
        bC.merge_from(bGb[0], bGb[1], bGb[2], bGb[3], bGhT, bHt2, bDtT)
        bW0.merge_from(bw1s[0], bw1s[1], bw2s[0])
        bW1.merge_from(bw2s[1])
        bE.merge_from(*bPT)

    P.plan["pool"].append(("w", SO, SO.v))
    P.check()

    with ExitStack() as es:
        for nm in P.sems:
            P.sems[nm].h = es.enter_context(nc.semaphore(nm))
        block = es.enter_context(nc.Block())
        block.tensor(lambda EG: P.run("pe", EG))
        block.scalar(lambda EG: P.run("act", EG))
        block.vector(lambda EG: P.run("dve", EG))
        block.gpsimd(lambda EG: P.run("pool", EG))
        block.sync(lambda EG: P.run("sp", EG))
    return nc


def kernel(L0_attn_x, L0_attn_Wq, L0_attn_Wk, L0_attn_Wv, L0_attn_Wo,
           L0_ffn_centroids, L0_ffn_gamma, L0_ffn_beta, L0_ffn_W1, L0_ffn_b1,
           L0_ffn_W2, L0_ffn_b2):
    x = np.ascontiguousarray(np.asarray(L0_attn_x, dtype=np.float32))
    Wq = np.ascontiguousarray(np.asarray(L0_attn_Wq, dtype=np.float32))
    Wk = np.ascontiguousarray(np.asarray(L0_attn_Wk, dtype=np.float32))
    Wv = np.ascontiguousarray(np.asarray(L0_attn_Wv, dtype=np.float32))
    Wo = np.ascontiguousarray(np.asarray(L0_attn_Wo, dtype=np.float32))
    cen = np.asarray(L0_ffn_centroids, dtype=np.float32)
    gam = np.asarray(L0_ffn_gamma, dtype=np.float32)
    bet = np.asarray(L0_ffn_beta, dtype=np.float32)
    W1 = np.asarray(L0_ffn_W1, dtype=np.float32)
    b1 = np.asarray(L0_ffn_b1, dtype=np.float32)
    W2 = np.asarray(L0_ffn_W2, dtype=np.float32)
    b2 = np.asarray(L0_ffn_b2, dtype=np.float32)

    w1h = np.ascontiguousarray(
        W1.reshape(E, 16, 256, 8, 128).transpose(0, 1, 4, 3, 2)).reshape(E, 16, 128, 2048)
    w2h = np.ascontiguousarray(
        W2.reshape(E, 8, 128, 32, 128).transpose(0, 1, 4, 3, 2)).reshape(E, 8, 128, 4096)
    b1h = np.ascontiguousarray(b1.reshape(E, 32, 128).transpose(2, 0, 1)).reshape(128, E * 32)
    b2h = np.ascontiguousarray(b2.reshape(E, 8, 128).transpose(2, 0, 1)).reshape(128, E * 8)
    gamh = np.ascontiguousarray(gam.reshape(E, 8, 128).transpose(2, 0, 1)).reshape(128, E * 8)
    beth = np.ascontiguousarray(bet.reshape(E, 8, 128).transpose(2, 0, 1)).reshape(128, E * 8)
    cenh = np.ascontiguousarray(cen.T.reshape(8, 128, 8).transpose(1, 0, 2)).reshape(128, 64)

    idfc = np.eye(128, dtype=np.float32)
    idbc = np.eye(128, dtype=np.float32).astype(ml_dtypes.bfloat16)
    mkbc = (np.arange(896)[None, :] >= (np.arange(128)[:, None] + 384)).astype(
        ml_dtypes.bfloat16)
    onec = np.ones((1, 64), dtype=np.float32)
    ltc = (np.arange(128)[:, None] < np.arange(128)[None, :]).astype(np.float32)
    o1rc = np.ones((1, 128), dtype=np.float32)
    o1cc = np.ones((128, 1), dtype=np.float32)
    cvecc = (np.arange(8, dtype=np.float32) * CAP).reshape(1, 8)
    p2tic = np.full((NSLOT, 128), float(T), dtype=np.float32)
    tokfc = (np.arange(NT, dtype=np.float32)[None, :] * 128
             + np.arange(128, dtype=np.float32)[:, None])

    nc = build_nc()
    shared = dict(wq=Wq, wk=Wk, wv=Wv, wo=Wo, w1=w1h, w2=w2h, b1=b1h, b2=b2h,
                  gam=gamh, bet=beth, cen=cenh, idf=idfc, idb=idbc, mkb=mkbc,
                  ones=onec, lt=ltc, o1r=o1rc, o1c=o1cc, cvec=cvecc,
                  p2ti=p2tic, tokf=tokfc)
    in_maps = [dict(shared, x=np.ascontiguousarray(x[2 * c:2 * c + 2]))
               for c in range(8)]
    res = run_bass_kernel_spmd(nc, in_maps, core_ids=list(range(8)))
    return np.concatenate([r["out"] for r in res.results], axis=0)


# version 19 (final)
# speedup vs baseline: 1.0499x; 1.0424x over previous
import numpy as np
import ml_dtypes
from contextlib import ExitStack

import concourse.bass as bass
import concourse.mybir as mybir
from concourse.bass_utils import run_bass_kernel_spmd
from concourse.bass_core_rust import IndirectOffsetOnAxis

F32 = mybir.dt.float32
BF16 = mybir.dt.bfloat16
I32 = mybir.dt.int32
AF = mybir.ActivationFunctionType
ALU = mybir.AluOpType
AX = mybir.AxisListType

B_CORE = 2
T = 2048
D = 1024
H = 16
E = 8
NT = 16
EPS = 1e-5
VS = 80
CAP = 384
NG = CAP // 128
NSLOT = 48


class Sem:
    def __init__(self, name, step):
        self.name, self.step, self.v, self.h = name, step, 0, None


class Buf:
    __slots__ = ("name", "w", "r")

    def __init__(self, name):
        self.name, self.w, self.r = name, {}, {}

    def merge_from(self, *others):
        for o in others:
            for s, c in o.w.items():
                self.w[s] = max(self.w.get(s, 0), c)
            for s, c in o.r.items():
                self.r[s] = max(self.r.get(s, 0), c)


class Prog:
    def __init__(self):
        self.plan = {e: [] for e in ("pe", "act", "dve", "pool", "sp")}
        self.sems = {nm: Sem(nm, step) for nm, step in
                     (("spe", 1), ("sact", 1), ("sdve", 1), ("spool", 1),
                      ("sx0", 16), ("sx1", 16), ("so", 16), ("swc", 16),
                      ("swh0", 16), ("swh1", 16), ("sw10", 16), ("sw11", 16),
                      ("sw20", 16), ("sw21", 16), ("sp2t", 16), ("sln0", 16),
                      ("sln1", 16), ("sdel", 16), ("sidx", 16), ("spm", 16),
                      ("sg0", 16), ("sg1", 16), ("sg2", 16), ("sg3", 16),
                      ("sds", 16), ("sdl0", 16), ("sdl1", 16),
                      ("ssd0", 16), ("ssd1", 16), ("ssd2", 16), ("ssd3", 16))}
        self.engsem = {"pe": self.sems["spe"], "act": self.sems["sact"],
                       "dve": self.sems["sdve"], "pool": self.sems["spool"]}
        self.waited = {e: {} for e in self.plan}

    def wait(self, eng, sem, val):
        if val <= 0:
            return
        hw = self.waited[eng]
        if hw.get(sem, 0) >= val:
            return
        hw[sem] = val
        self.plan[eng].append(("w", sem, val))

    def emit(self, eng, fn, reads=(), writes=(), sem=None):
        if sem is None:
            sem = self.engsem[eng]
        need = {}
        for b in tuple(reads) + tuple(writes):
            for s, c in b.w.items():
                if s is not sem:
                    need[s] = max(need.get(s, 0), c)
        for b in writes:
            for s, c in b.r.items():
                if s is not sem:
                    need[s] = max(need.get(s, 0), c)
        for s, c in need.items():
            self.wait(eng, s, c)
        sem.v += sem.step
        self.plan[eng].append(("i", fn, sem, sem.step))
        for b in reads:
            b.r[sem] = sem.v
        for b in writes:
            b.w[sem] = sem.v
            b.r.pop(sem, None)
        return sem.v

    def check(self):
        for eng, hw in self.waited.items():
            for sem, val in hw.items():
                assert val <= sem.v, (eng, sem.name, val, sem.v)

    def run(self, eng, EG):
        for item in self.plan[eng]:
            if item[0] == "w":
                EG.wait_ge(item[1].h, item[2])
            else:
                item[1](EG).then_inc(item[2].h, item[3])


def build_nc(debug=False):
    nc = bass.Bass("TRN2", target_bir_lowering=False)

    x_d = nc.dram_tensor("x", [B_CORE, T, D], F32, kind="ExternalInput").ap()
    wq_d = nc.dram_tensor("wq", [D, D], F32, kind="ExternalInput").ap()
    wk_d = nc.dram_tensor("wk", [D, D], F32, kind="ExternalInput").ap()
    wv_d = nc.dram_tensor("wv", [D, D], F32, kind="ExternalInput").ap()
    wo_d = nc.dram_tensor("wo", [D, D], F32, kind="ExternalInput").ap()
    w1_d = nc.dram_tensor("w1", [E, 16, 128, 2048], F32, kind="ExternalInput").ap()
    w2_d = nc.dram_tensor("w2", [E, 8, 128, 4096], F32, kind="ExternalInput").ap()
    b1_d = nc.dram_tensor("b1", [128, E * 32], F32, kind="ExternalInput").ap()
    b2_d = nc.dram_tensor("b2", [128, E * 8], F32, kind="ExternalInput").ap()
    gam_d = nc.dram_tensor("gam", [128, E * 8], F32, kind="ExternalInput").ap()
    bet_d = nc.dram_tensor("bet", [128, E * 8], F32, kind="ExternalInput").ap()
    cen_d = nc.dram_tensor("cen", [128, 64], F32, kind="ExternalInput").ap()
    idf_d = nc.dram_tensor("idf", [128, 128], F32, kind="ExternalInput").ap()
    idb_d = nc.dram_tensor("idb", [128, 128], BF16, kind="ExternalInput").ap()
    mkb_d = nc.dram_tensor("mkb", [128, 896], BF16, kind="ExternalInput").ap()
    one_d = nc.dram_tensor("ones", [1, 64], F32, kind="ExternalInput").ap()
    lt_d = nc.dram_tensor("lt", [128, 128], F32, kind="ExternalInput").ap()
    o1r_d = nc.dram_tensor("o1r", [1, 128], F32, kind="ExternalInput").ap()
    o1c_d = nc.dram_tensor("o1c", [128, 1], F32, kind="ExternalInput").ap()
    cvec_d = nc.dram_tensor("cvec", [1, 8], F32, kind="ExternalInput").ap()
    p2ti_d = nc.dram_tensor("p2ti", [NSLOT, 128], F32, kind="ExternalInput").ap()
    tokf_d = nc.dram_tensor("tokf", [128, NT], F32, kind="ExternalInput").ap()
    out_d = nc.dram_tensor("out", [B_CORE, T, D], F32, kind="ExternalOutput").ap()
    ln2_dram = nc.dram_tensor("ln2sc", [T + 1, D], BF16).ap()
    del_dram = nc.dram_tensor("delsc", [T + 1, D], BF16).ap()
    p2t_dram = nc.dram_tensor("p2tsc", [NSLOT, 128], F32).ap()
    s_dram = nc.dram_tensor("ssc", [T, D], BF16).ap()
    dbg_t = {}

    bufA = nc.alloc_sbuf_tensor("bufA", [128, 16384], BF16)
    bufB = nc.alloc_sbuf_tensor("bufB", [128, 16384], BF16)
    bufC = nc.alloc_sbuf_tensor("bufC", [128, 16 * 16 * 80], BF16)
    bufD = nc.alloc_sbuf_tensor("bufD", [128, 16384], BF16)
    bufE = nc.alloc_sbuf_tensor("bufE", [128, 4096], BF16)
    wbuf = nc.alloc_sbuf_tensor("wbuf", [128, 16384], BF16)
    mkb = nc.alloc_sbuf_tensor("mkb_s", [128, 896], BF16)
    idf = nc.alloc_sbuf_tensor("idf_s", [128, 128], F32)
    idb = nc.alloc_sbuf_tensor("idb_s", [128, 128], BF16)
    onesc = nc.alloc_sbuf_tensor("ones_s", [1, 64], F32)
    ltsb = nc.alloc_sbuf_tensor("lt_s", [128, 128], F32)
    o1r = nc.alloc_sbuf_tensor("o1r_s", [1, 128], F32)
    o1c = nc.alloc_sbuf_tensor("o1c_s", [128, 1], F32)
    cvecs = nc.alloc_sbuf_tensor("cvec_s", [1, 8], F32)
    cumrow = nc.alloc_sbuf_tensor("cum_s", [1, 8], F32)
    ctmp = nc.alloc_sbuf_tensor("ctmp_s", [1, 8], F32)
    rtsb = nc.alloc_sbuf_tensor("rt_s", [128, 8], F32)
    selr = nc.alloc_sbuf_tensor("selr_s", [128, 8], F32)
    dest_f = nc.alloc_sbuf_tensor("destf", [128, NT], F32)
    dest_i = nc.alloc_sbuf_tensor("desti", [128, NT], I32)
    tokf_sb = nc.alloc_sbuf_tensor("tokf_s", [128, NT], F32)
    pmcm = nc.alloc_sbuf_tensor("pmcm", [NSLOT, 128], F32)
    posm_f = nc.alloc_sbuf_tensor("posmf", [128, NSLOT], F32)
    posm_i = nc.alloc_sbuf_tensor("posmi", [128, NSLOT], I32)
    dtok2 = nc.alloc_sbuf_tensor("dtok2", [128, 1024], BF16)
    b1sb = nc.alloc_sbuf_tensor("b1_s", [128, E * 32], F32)
    b2sb = nc.alloc_sbuf_tensor("b2_s", [128, E * 8], F32)
    gamsb = nc.alloc_sbuf_tensor("gam_s", [128, E * 8], F32)
    betsb = nc.alloc_sbuf_tensor("bet_s", [128, E * 8], F32)
    cenT = nc.alloc_sbuf_tensor("cen_s", [128, 64], F32)
    xf = nc.alloc_sbuf_tensor("xf", [128, 2048], F32)
    tfA = nc.alloc_sbuf_tensor("tfA", [128, 1024], F32)
    tfA2 = nc.alloc_sbuf_tensor("tfA2", [128, 1024], F32)
    dtok = nc.alloc_sbuf_tensor("dtok", [128, 1024], BF16)
    scsb = nc.alloc_sbuf_tensor("scsb", [128, 16], F32)
    mks = nc.alloc_sbuf_tensor("mks", [128, 128], F32)
    lrep = nc.alloc_sbuf_tensor("lrep", [64, 1024], BF16)
    STN = ("s1", "s2", "mu1", "va", "rstd1", "rtmp", "std1", "s1b", "s2b",
           "mu2", "sq2", "c1", "c2", "tq", "mmax", "ea", "alpha")
    WIDE = ("mu1", "rstd1", "alpha", "c1", "c2")
    st = {k: nc.alloc_sbuf_tensor("st_" + k, [128, 2 * NT if k in WIDE else NT], F32)
          for k in STN}
    eps_sb = nc.alloc_sbuf_tensor("eps_s", [128, 1], F32)

    psSa = nc.alloc_psum_tensor("psSa", [128, 1024], F32)
    psSb = nc.alloc_psum_tensor("psSb", [128, 1024], F32)
    psE = nc.alloc_psum_tensor("psE", [128, 512], F32)
    psF = nc.alloc_psum_tensor("psF", [128, 512], F32)
    psT = nc.alloc_psum_tensor("psT", [128, 2048], BF16)

    P = Prog()
    SX = [P.sems["sx0"], P.sems["sx1"]]
    SO, SWC = P.sems["so"], P.sems["swc"]
    SWH = [P.sems["swh0"], P.sems["swh1"]]
    SW1 = [P.sems["sw10"], P.sems["sw11"]]
    SW2 = [P.sems["sw20"], P.sems["sw21"]]
    SP2T, SDEL, SIDX, SPM, SDS = (P.sems["sp2t"], P.sems["sdel"],
                                  P.sems["sidx"], P.sems["spm"], P.sems["sds"])
    SLN = [P.sems["sln0"], P.sems["sln1"]]
    SDL = [P.sems["sdl0"], P.sems["sdl1"]]
    SSD = [P.sems["ssd%d" % i] for i in range(4)]

    bA, bB, bC, bD, bE = Buf("A"), Buf("B"), Buf("C"), Buf("D"), Buf("E")
    bA4 = [Buf("A4_%d" % i) for i in range(4)]
    bCln, bCh = Buf("Cln"), Buf("Ch")
    bW0, bW1 = Buf("W0"), Buf("W1")
    bw1s = [Buf("w1s0"), Buf("w1s1")]
    bw2s = [Buf("w2s0"), Buf("w2s1")]
    bPT = [Buf("PT0"), Buf("PT1"), Buf("PT2"), Buf("PT3")]
    bXF = [Buf("xf0"), Buf("xf1")]
    bTfA, bTfA2, bDtok = Buf("tfA"), Buf("tfA2"), Buf("dtok")
    bSc = [Buf("sc0"), Buf("sc1")]
    bMk = Buf("mks")
    bCum, bRt, bSelr = Buf("cum"), Buf("rt"), Buf("selr")
    bCtmp = Buf("ctmp")
    bDest, bDestI = Buf("dest"), Buf("desti")
    bP2T, bLn2D, bDelD = Buf("p2t"), Buf("ln2d"), Buf("deld")
    bPmcm, bPosmF, bPosmI = Buf("pmcm"), Buf("posmf"), Buf("posmi")
    bSd = Buf("sdram")
    bGb = [Buf("gb0"), Buf("gb1"), Buf("gb2"), Buf("gb3")]
    bGhT, bHt2, bDtT, bZl = Buf("ghT"), Buf("ht2"), Buf("dtT"), Buf("zl")
    bDt2 = Buf("dtok2")
    bLrep = Buf("lrep")
    bST = {k: Buf("st_" + k) for k in STN}
    bConst = Buf("const")
    bSa0, bSa1 = Buf("psSa0"), Buf("psSa1")
    bSb0, bSb1 = Buf("psSb0"), Buf("psSb1")
    bPsE, bPsF = Buf("psE"), Buf("psF")
    bPsT = [Buf("psT0"), Buf("psT1")]
    bOutT = Buf("outT")

    mm1rot = [(psSa, 0, bSa0), (psSa, 512, bSa1), (psSb, 0, bSb0), (psSb, 512, bSb1)]

    def dma(eng, dst_ap, src_ap, wbufs, rbufs, sem):
        P.emit(eng, lambda EG, d=dst_ap, s=src_ap: EG.dma_start(out=d, in_=s),
               reads=rbufs, writes=wbufs, sem=sem)

    def dump(name, ap, shape, dtype, rbufs):
        if not debug:
            return
        dd = nc.dram_tensor(name, list(shape), dtype, kind="ExternalOutput").ap()
        dbg_t[name] = dd
        dma("sp", dd, ap, [], rbufs, P.sems["so"])

    for ci, (dst_t, src) in enumerate(
            ((mkb, mkb_d), (idf, idf_d), (idb, idb_d),
             (onesc, one_d), (b1sb, b1_d), (b2sb, b2_d),
             (gamsb, gam_d), (betsb, bet_d), (cenT, cen_d),
             (ltsb, lt_d), (o1r, o1r_d), (o1c, o1c_d), (cvecs, cvec_d),
             (tokf_sb, tokf_d))):
        P.wait("pool", SWC, 16 * ci)
        dma("pool", dst_t[:, :], src, [bConst], [], SWC)

    P.emit("dve", lambda EG: EG.memset(eps_sb[:, :], EPS), reads=[], writes=[bConst])

    def mm_group(out_ap, pairs, reads, writes):
        def fn(EG, o=out_ap, ps=tuple(pairs)):
            last = None
            n = len(ps)
            for i, (l, r) in enumerate(ps):
                last = EG.matmul(o, l, r, start=(i == 0), stop=(i == n - 1))
            return last
        return P.emit("pe", fn, reads=reads, writes=writes)

    def phase0_closures(b):
        ops = []
        def _pre(b=b):
            dma("pool", p2t_dram, p2ti_d, [bP2T], [], SP2T)
            P.emit("dve", lambda EG: EG.memset(cumrow[0:1, :], 0.0),
                   reads=[], writes=[bCum])
        ops.append(_pre)
        for t in range(NT):
            def _tile(b=b, t=t):
                sl = t % 2
                xfs = xf[:, sl * 1024:sl * 1024 + 1024]
                bxf = bXF[sl]
                if t == 0:
                    dma("sp", xfs, x_d[b, 0:128, :], [bxf], [], SX[0])
                if t + 1 < NT:
                    sl2 = (t + 1) % 2
                    dma("sp", xf[:, sl2 * 1024:sl2 * 1024 + 1024],
                        x_d[b, (t + 1) * 128:(t + 2) * 128, :], [bXF[sl2]], [], SX[sl2])
                P.emit("dve", lambda EG, i=xfs, o=st["s1"][:, t:t + 1]:
                       EG.reduce_sum(o, i, axis=AX.X), reads=[bxf], writes=[bST["s1"]])
                P.emit("act", lambda EG, i=xfs, a=st["s2"][:, t:t + 1]:
                       EG.activation(bufE[:, 0:1024], i, AF.Square, accum_out=a),
                       reads=[bxf], writes=[bPT[0], bST["s2"]])
                bc0 = b * NT
                P.emit("act", lambda EG, t=t, bc0=b * NT: EG.mul(st["mu1"][:, bc0 + t:bc0 + t + 1],
                                                     st["s1"][:, t:t + 1], 1.0 / D),
                       reads=[bST["s1"]], writes=[bST["mu1"]])
                P.emit("pool", lambda EG, t=t, bc0=b * NT: EG.tensor_tensor(
                    st["tq"][:, t:t + 1], st["mu1"][:, bc0 + t:bc0 + t + 1],
                    st["mu1"][:, bc0 + t:bc0 + t + 1], ALU.mult),
                    reads=[bST["mu1"]], writes=[bST["tq"]])
                P.emit("dve", lambda EG, t=t: EG.scalar_tensor_tensor(
                    st["va"][:, t:t + 1], st["s2"][:, t:t + 1], 1.0 / D,
                    st["tq"][:, t:t + 1], ALU.mult, ALU.subtract),
                    reads=[bST["s2"], bST["tq"]], writes=[bST["va"]])
                P.emit("act", lambda EG, t=t: EG.activation(
                    st["std1"][:, t:t + 1], st["va"][:, t:t + 1], AF.Sqrt,
                    bias=eps_sb[:, 0:1]),
                    reads=[bST["va"], bConst], writes=[bST["std1"]])
                P.emit("dve", lambda EG, t=t: EG.reciprocal(
                    st["rtmp"][:, t:t + 1], st["std1"][:, t:t + 1]),
                    reads=[bST["std1"]], writes=[bST["rtmp"]])
                P.emit("pool", lambda EG, t=t, bc0=b * NT: EG.tensor_copy(
                    st["rstd1"][:, bc0 + t:bc0 + t + 1], st["rtmp"][:, t:t + 1]),
                    reads=[bST["rtmp"]], writes=[bST["rstd1"]])
                tfX = tfA if t % 2 == 0 else tfA2
                bTfX = bTfA if t % 2 == 0 else bTfA2
                P.emit("dve", lambda EG, i=xfs, t=t, o=tfX, bc0=b * NT:
                       EG.tensor_scalar(o[:, :], i, st["mu1"][:, bc0 + t:bc0 + t + 1],
                                        st["rstd1"][:, bc0 + t:bc0 + t + 1],
                                        ALU.subtract, ALU.mult),
                       reads=[bxf, bST["mu1"], bST["rstd1"]], writes=[bTfX])
                dslot0 = bufE[:, (1 + t % 2) * 1024:(2 + t % 2) * 1024]
                bds0 = bPT[1 + t % 2]
                P.emit("act", lambda EG, i=tfX, o=dslot0: EG.copy(o, i[:, :]),
                       reads=[bTfX], writes=[bds0])
                P.emit("dve", lambda EG, o=st["s1b"][:, t:t + 1], i=tfX:
                       EG.reduce_sum(o, i[:, :], axis=AX.X),
                       reads=[bTfX], writes=[bST["s1b"]])
                P.emit("act", lambda EG, a=st["s2b"][:, t:t + 1], i=tfX:
                       EG.activation(bufE[:, 0:1024], i[:, :], AF.Square, accum_out=a),
                       reads=[bTfX], writes=[bPT[0], bST["s2b"]])
                P.emit("act", lambda EG, t=t: EG.mul(st["mu2"][:, t:t + 1],
                                                     st["s1b"][:, t:t + 1], 1.0 / D),
                       reads=[bST["s1b"]], writes=[bST["mu2"]])
                P.emit("pool", lambda EG, t=t: EG.tensor_tensor(
                    st["tq"][:, t:t + 1], st["mu2"][:, t:t + 1],
                    st["mu2"][:, t:t + 1], ALU.mult),
                    reads=[bST["mu2"]], writes=[bST["tq"]])
                P.emit("dve", lambda EG, t=t: EG.scalar_tensor_tensor(
                    st["va"][:, t:t + 1], st["s2b"][:, t:t + 1], 1.0 / D,
                    st["tq"][:, t:t + 1], ALU.mult, ALU.subtract),
                    reads=[bST["s2b"], bST["tq"]], writes=[bST["va"]])
                P.emit("act", lambda EG, t=t: EG.activation(
                    st["sq2"][:, t:t + 1], st["va"][:, t:t + 1], AF.Sqrt,
                    bias=eps_sb[:, 0:1]),
                    reads=[bST["va"], bConst], writes=[bST["sq2"]])
                P.emit("pool", lambda EG, t=t: EG.tensor_tensor(
                    st["ea"][:, t:t + 1], st["std1"][:, t:t + 1],
                    st["sq2"][:, t:t + 1], ALU.mult),
                    reads=[bST["std1"], bST["sq2"]], writes=[bST["ea"]])
                P.emit("dve", lambda EG, t=t, bc0=b * NT: EG.reciprocal(
                    st["c2"][:, bc0 + t:bc0 + t + 1], st["ea"][:, t:t + 1]),
                    reads=[bST["ea"]], writes=[bST["c2"]])
                P.emit("dve", lambda EG, t=t, bc0=b * NT: EG.scalar_tensor_tensor(
                    st["c1"][:, bc0 + t:bc0 + t + 1], st["mu2"][:, t:t + 1],
                    st["std1"][:, t:t + 1],
                    st["mu1"][:, bc0 + t:bc0 + t + 1], ALU.mult, ALU.add),
                    reads=[bST["mu2"], bST["std1"], bST["mu1"]], writes=[bST["c1"]])
                half = t % 2
                pshalf = psT[:, half * 1024:half * 1024 + 1024]
                def _trb(EG, ph=pshalf, i=dslot0):
                    last = None
                    for c in range(8):
                        last = EG.transpose(ph[:, c * 128:(c + 1) * 128],
                                            i[:, c * 128:(c + 1) * 128], idb[:, :])
                    return last
                P.emit("pe", _trb, reads=[bds0, bConst], writes=[bPsT[half]])
                dstA = bufA[:, 0:8 * T].rearrange("p (c q) -> p c q", c=8)[:, :, t * 128:(t + 1) * 128]
                srcT = pshalf.rearrange("p (c q) -> p c q", c=8)
                P.emit("dve", lambda EG, d2=dstA, s2=srcT: EG.tensor_copy(d2, s2),
                       reads=[bPsT[half]], writes=[bA4[t // 4]])
                for c in range(8):
                    ps, pb = (psE, bPsE) if c % 2 == 0 else (psF, bPsF)
                    P.emit("pe", lambda EG, c=c, ps=ps, i=tfX:
                           EG.transpose(ps[:, 0:128], i[:, c * 128:(c + 1) * 128], idf[:, :]),
                           reads=[bTfX, bConst], writes=[pb])
                    P.emit("act", lambda EG, c=c, ps=ps, o=xfs:
                           EG.copy(o[:, c * 128:(c + 1) * 128], ps[:, 0:128]),
                           reads=[pb], writes=[bxf])
                mm_group(psSa[:, 0:8],
                         [(xfs[:, c * 128:(c + 1) * 128], cenT[:, c * 8:(c + 1) * 8])
                          for c in range(8)],
                         reads=[bxf, bConst], writes=[bSa0])
                ssl = scsb[:, sl * 8:sl * 8 + 8]
                msl = mks[:, t * 8:t * 8 + 8]
                P.emit("act", lambda EG, o=ssl: EG.copy(o, psSa[:, 0:8]),
                       reads=[bSa0], writes=[bSc[sl]])
                P.emit("dve", lambda EG, i=ssl, t=t:
                       EG.reduce_max(st["mmax"][:, t:t + 1], i, axis=AX.X),
                       reads=[bSc[sl]], writes=[bST["mmax"]])
                P.emit("pool", lambda EG, i=ssl, o=msl, t=t:
                       EG.tensor_single_scalar(o, i, st["mmax"][:, t:t + 1], ALU.is_equal),
                       reads=[bSc[sl], bST["mmax"]], writes=[bMk])
                def _rank(EG, m=msl):
                    EG.matmul(psSb[:, 0:8], ltsb[:, :], m, start=True, stop=False)
                    EG.matmul(psSb[:, 0:8], o1r[0:1, :], cumrow[0:1, :], start=False, stop=False)
                    return EG.matmul(psSb[:, 0:8], o1r[0:1, :], cvecs[0:1, :], start=False, stop=True)
                P.emit("pe", _rank, reads=[bConst, bMk, bCum], writes=[bSb0])
                P.emit("pe", lambda EG, m=msl: EG.matmul(
                    psSb[0:1, 512:520], o1c[:, :], m, start=True, stop=True),
                    reads=[bConst, bMk], writes=[bSb1])
                P.emit("act", lambda EG: EG.copy(ctmp[0:1, :], psSb[0:1, 512:520]),
                       reads=[bSb1], writes=[bCtmp])
                P.emit("pool", lambda EG: EG.tensor_tensor(
                    cumrow[0:1, :], cumrow[0:1, :], ctmp[0:1, :], ALU.add),
                    reads=[bCtmp, bCum], writes=[bCum])
                P.emit("act", lambda EG: EG.copy(rtsb[:, :], psSb[:, 0:8]),
                       reads=[bSb0], writes=[bRt])
                P.emit("pool", lambda EG, m=msl: EG.tensor_tensor(selr[:, :], m, rtsb[:, :], ALU.mult),
                       reads=[bMk, bRt], writes=[bSelr])
                P.emit("dve", lambda EG, t=t: EG.reduce_sum(
                    dest_f[:, t:t + 1], selr[:, :], axis=AX.X),
                    reads=[bSelr], writes=[bDest])

            ops.append(_tile)
        def _post(b=b):
            P.emit("act", lambda EG: EG.activation(st["ea"][:, :], st["mmax"][:, :],
                                                   AF.Exp, scale=-1.0),
                   reads=[bST["mmax"]], writes=[bST["ea"]])
            P.emit("pool", lambda EG: EG.tensor_scalar(
                st["tq"][:, :], st["ea"][:, :], 1.0, None, ALU.add),
                reads=[bST["ea"]], writes=[bST["tq"]])
            P.emit("dve", lambda EG, bc0=b * NT: EG.reciprocal(
                st["alpha"][:, bc0:bc0 + NT], st["tq"][:, :]),
                reads=[bST["tq"]], writes=[bST["alpha"]])
            P.emit("dve", lambda EG: EG.tensor_copy(dest_i[:, :], dest_f[:, :]),
                   reads=[bDest], writes=[bDestI])
            p2t_flat = p2t_dram.rearrange("a (b c) -> (a b) c", c=1)
            for t in range(NT):
                P.emit("pool", lambda EG, t=t: EG.indirect_dma_start(
                    out=p2t_flat,
                    out_offset=IndirectOffsetOnAxis(ap=dest_i[:, t:t + 1], axis=0),
                    in_=tokf_sb[:, t:t + 1], in_offset=None),
                    reads=[bDestI, bConst, bP2T], writes=[bP2T], sem=SIDX)
            dma("pool", pmcm[:, :], p2t_dram, [bPmcm], [bP2T], SPM)
            P.emit("pe", lambda EG: EG.transpose(psE[:, 0:NSLOT], pmcm[:, :],
                                                 idf[0:NSLOT, 0:NSLOT]),
                   reads=[bPmcm, bConst], writes=[bPsE])
            P.emit("act", lambda EG: EG.copy(posm_f[:, :], psE[:, 0:NSLOT]),
                   reads=[bPsE], writes=[bPosmF])
            P.emit("dve", lambda EG: EG.tensor_copy(posm_i[:, :], posm_f[:, :]),
                   reads=[bPosmF], writes=[bPosmI])

            if debug and b == 0:
                for nm in ("mu1", "rstd1", "c1", "c2", "alpha", "mmax",
                           "s1", "s2", "va", "tq", "std1", "s1b", "s2b",
                           "mu2", "sq2", "ea"):
                    dump("d_" + nm, st[nm][:, 0:NT], (128, NT), F32, [bST[nm]])
                dump("d_xnT", bufA[:, :], (128, 16384), BF16, [bA])
                dump("d_dest", dest_f[:, :], (128, NT), F32, [bDest])
                dump("d_posm", posm_f[:, :], (128, NSLOT), F32, [bPosmF])

        ops.append(_post)
        return ops

    pending_final = []
    pending_p0 = []
    pending_p0_post = []
    for b in range(B_CORE):
        if b == 0:
            for _c in phase0_closures(0):
                _c()
        while pending_p0:
            pending_p0.pop(0)()
        while pending_p0_post:
            pending_p0_post.pop(0)()
        def load_w(dram, half, bhalf):
            for k in range(8):
                P.wait("pool", SWH[half], SWH[half].v)
                dma("pool",
                    wbuf[:, half * 8192 + k * 1024: half * 8192 + (k + 1) * 1024],
                    dram[k * 128:(k + 1) * 128, :], [bhalf], [], SWH[half])
        load_w(wq_d, 0, bW0)
        load_w(wk_d, 1, bW1)

        for (wsl, bws, dstbuf, bdst) in ((0, bW0, bufD, bD), (1, bW1, bufB, bB)):
            for c in range(8):
                for q4 in range(4):
                    pst, off, bbank = mm1rot[(c * 4 + q4) % 4]
                    out_ap = pst[:, off:off + 512]
                    pairs = [(wbuf[:, wsl * 8192 + k * 1024 + c * 128:
                                   wsl * 8192 + k * 1024 + c * 128 + 128],
                              bufA[:, k * T + q4 * 512: k * T + q4 * 512 + 512])
                             for k in range(8)]
                    mm_group(out_ap, pairs, reads=[bws, bA4[q4]], writes=[bbank])
                    P.emit("dve", lambda EG,
                           o=dstbuf[:, c * T + q4 * 512: c * T + q4 * 512 + 512],
                           i=out_ap: EG.tensor_copy(o, i),
                           reads=[bbank], writes=[bdst])
                    if pending_final:
                        pending_final.pop(0)()
                    if pending_final:
                        pending_final.pop(0)()
        ones_ap = bufC[:, 0:16 * 16 * VS].rearrange(
            "p (kt h j) -> p kt h j", h=H, j=VS)[:, :, :, 64:65]
        P.emit("dve", lambda EG, a=ones_ap: EG.memset(a, 1.0),
               reads=[], writes=[bC])
        load_w(wv_d, 0, bW0)
        for kt in range(16):
            for hf in range(2):
                pst, off, bbank = mm1rot[(kt * 2 + hf) % 4]
                out_ap = pst[:, off:off + 512]
                pairs = [(bufA[:, k * T + kt * 128: k * T + (kt + 1) * 128],
                          wbuf[:, k * 1024 + hf * 512: k * 1024 + hf * 512 + 512])
                         for k in range(8)]
                mm_group(out_ap, pairs, reads=[bA4[kt // 4], bW0], writes=[bbank])
                if pending_final:
                    pending_final.pop(0)()
                if pending_final:
                    pending_final.pop(0)()
                dstV = bufC[:, kt * 1280 + hf * 8 * VS: kt * 1280 + (hf + 1) * 8 * VS]\
                    .rearrange("p (h j) -> p h j", j=VS)[:, :, 0:64]
                srcV = out_ap.rearrange("p (h j) -> p h j", j=64)
                P.emit("dve", lambda EG, d2=dstV, s2=srcV: EG.tensor_copy(d2, s2),
                       reads=[bbank], writes=[bC])

        if debug and b == 0:
            dump("d_qt", bufD[:, :], (128, 16384), BF16, [bD])
            dump("d_kt", bufB[:, :], (128, 16384), BF16, [bB])
            dump("d_vp", bufC[:, :], (128, 16 * 16 * 80), BF16, [bC])

        for _pt in bPT:
            _pt.merge_from(bE)
        bA.merge_from(*bA4)
        load_w(wo_d, 1, bW1)

        while pending_final:
            pending_final.pop(0)()

        pending_tail = [None]

        def emit_tail(hp, qg):
            def tail():
                def _ln(EG):
                    EG.activation(tfA[32:33, 0:512], psE[64:65, :], AF.Ln)
                    return EG.activation(tfA[32:33, 512:1024], psF[64:65, :], AF.Ln)
                P.emit("act", _ln, reads=[bPsE, bPsF], writes=[bTfA])
                P.emit("act", lambda EG: EG.activation(
                    tfA[0:1, 0:1024], tfA[32:33, 0:1024], AF.Exp, scale=-1.0),
                    reads=[bTfA], writes=[bTfA])
                def _lbc(EG):
                    EG.matmul(psE[64:128, :], onesc[0:1, :], tfA[0:1, 0:512],
                              start=True, stop=True)
                    return EG.matmul(psF[64:128, :], onesc[0:1, :],
                                     tfA[0:1, 512:1024], start=True, stop=True)
                P.emit("pe", _lbc, reads=[bTfA, bConst], writes=[bPsE, bPsF])
                def _lrp(EG):
                    EG.copy(lrep[0:64, 0:512], psE[64:128, :])
                    return EG.copy(lrep[0:64, 512:1024], psF[64:128, :])
                P.emit("act", _lrp, reads=[bPsE, bPsF], writes=[bLrep])
                def _cev(EG, hp=hp, qg=qg):
                    EG.tensor_tensor(
                        bufA[0:64, hp * T + qg * 512: hp * T + qg * 512 + 512],
                        psE[0:64, :], lrep[0:64, 0:512], ALU.mult)
                    return EG.tensor_tensor(
                        bufA[64:128, hp * T + qg * 512: hp * T + qg * 512 + 512],
                        psF[0:64, :], lrep[0:64, 512:1024], ALU.mult)
                P.emit("dve", _cev, reads=[bPsE, bPsF, bLrep], writes=[bA])
            return tail

        for hp in range(8):
            for qg in (3, 2, 1, 0):
                ktmax = 4 * qg + 4

                def emit_S(kt, hp=hp, qg=qg):
                    pair = psSb if kt % 2 == 0 else psSa
                    bp = (bSb0, bSb1) if kt % 2 == 0 else (bSa0, bSa1)
                    def _sp(EG, pp=pair, hp=hp, qg=qg, kt=kt):
                        EG.matmul(pp[:, 0:512],
                                  bufB[0:64, hp * T + kt * 128: hp * T + (kt + 1) * 128],
                                  bufD[0:64, hp * T + qg * 512: hp * T + qg * 512 + 512],
                                  start=True, stop=True, tile_position=(0, 0))
                        return EG.matmul(
                            pp[:, 512:1024],
                            bufB[64:128, hp * T + kt * 128: hp * T + (kt + 1) * 128],
                            bufD[64:128, hp * T + qg * 512: hp * T + qg * 512 + 512],
                            start=True, stop=True, tile_position=(64, 0))
                    P.emit("pe", _sp, reads=[bB, bD], writes=list(bp))
                    sl = kt % 4
                    P.emit("act", lambda EG, pp=pair, sl=sl:
                           EG.activation(bufE[:, sl * 1024: sl * 1024 + 1024],
                                         pp[:, 0:1024], AF.Exp, scale=0.125),
                           reads=list(bp), writes=[bPT[sl]])
                    j = kt - 4 * qg
                    if j >= 0:
                        def _mask(EG, sl=sl, j=j):
                            EG.tensor_tensor(bufE[:, sl * 1024: sl * 1024 + 512],
                                             bufE[:, sl * 1024: sl * 1024 + 512],
                                             mkb[:, 384 - 128 * j: 896 - 128 * j], ALU.mult)
                            return EG.tensor_tensor(
                                bufE[:, sl * 1024 + 512: sl * 1024 + 1024],
                                bufE[:, sl * 1024 + 512: sl * 1024 + 1024],
                                mkb[:, 384 - 128 * j: 896 - 128 * j], ALU.mult)
                        P.emit("dve", _mask, reads=[bConst], writes=[bPT[sl]])

                def emit_ctx(kt, hp=hp, ktmax=ktmax):
                    sl = kt % 4
                    def _ctx(EG, hp=hp, kt=kt, sl=sl,
                             first=(kt == 0), last=(kt == ktmax - 1)):
                        EG.matmul(psE[0:65, :],
                                  bufC[:, kt * 1280 + (2 * hp) * VS:
                                       kt * 1280 + (2 * hp) * VS + 65],
                                  bufE[:, sl * 1024: sl * 1024 + 512],
                                  start=first, stop=last)
                        return EG.matmul(psF[0:65, :],
                                         bufC[:, kt * 1280 + (2 * hp + 1) * VS:
                                              kt * 1280 + (2 * hp + 1) * VS + 65],
                                         bufE[:, sl * 1024 + 512: sl * 1024 + 1024],
                                         start=first, stop=last)
                    P.emit("pe", _ctx, reads=[bC, bPT[sl]], writes=[bPsE, bPsF])

                for kt in range(ktmax):
                    emit_S(kt)
                    if kt == 2 and pending_tail[0] is not None:
                        pending_tail[0]()
                        pending_tail[0] = None
                    if kt >= 3:
                        emit_ctx(kt - 3)
                if pending_tail[0] is not None:
                    pending_tail[0]()
                    pending_tail[0] = None
                for kt in range(max(0, ktmax - 3), ktmax):
                    emit_ctx(kt)
                pending_tail[0] = emit_tail(hp, qg)
        pending_tail[0]()
        pending_tail[0] = None

        if debug and b == 0:
            dump("d_ctx", bufA[:, :], (128, 16384), BF16, [bA])

        for qt in range(16):
            for dh in range(2):
                pst, pb = (psE, bPsE) if (qt * 2 + dh) % 2 == 0 else (psF, bPsF)
                pairs = [(bufA[:, c * T + qt * 128: c * T + (qt + 1) * 128],
                          wbuf[:, 8192 + c * 1024 + dh * 512:
                               8192 + c * 1024 + dh * 512 + 512])
                         for c in range(8)]
                mm_group(pst[:, :], pairs, reads=[bA, bW1], writes=[pb])
                ssl_ = qt % 4
                sap = bufE[:, ssl_ * 1024 + dh * 512: ssl_ * 1024 + dh * 512 + 512]
                P.emit("dve", lambda EG, o=sap, i=pst[:, :]: EG.tensor_copy(o, i),
                       reads=[pb], writes=[bPT[ssl_]])
                dma("sp", s_dram[qt * 128:(qt + 1) * 128, dh * 512:dh * 512 + 512],
                    sap, [bSd], [bPT[ssl_]], SSD[ssl_])

        if debug and b == 0:
            dump("d_s", bufB[:, :], (128, 16384), BF16, [bB])

        bZl.merge_from(bC)
        bGb[0].merge_from(bC)
        bGb[1].merge_from(bC)
        bGhT.merge_from(bC)
        bHt2.merge_from(bC)
        bDtT.merge_from(bC)
        for s_ in bw1s + bw2s:
            s_.merge_from(bW0, bW1)

        P.emit("dve", lambda EG: EG.memset(bufC[:, 0:1024], 0.0),
               reads=[], writes=[bGb[0]])
        for t in range(NT):
            sl = t % 2
            xfs = xf[:, sl * 1024:sl * 1024 + 1024]
            bxf = bXF[sl]
            if t == 0:
                dma("sp", xfs, x_d[b, 0:128, :], [bxf], [], SX[0])
            if t + 1 < NT:
                sl2 = (t + 1) % 2
                dma("sp", xf[:, sl2 * 1024:sl2 * 1024 + 1024],
                    x_d[b, (t + 1) * 128:(t + 2) * 128, :], [bXF[sl2]], [], SX[sl2])
            dslot = dtok if sl == 0 else dtok2
            bdsl = bDtok if sl == 0 else bDt2
            P.emit("dve", lambda EG, i=xfs, t=t, o=dslot, bc0=b * NT:
                   EG.tensor_scalar(o[:, :], i, st["c1"][:, bc0 + t:bc0 + t + 1],
                                    st["c2"][:, bc0 + t:bc0 + t + 1],
                                    ALU.subtract, ALU.mult),
                   reads=[bxf, bST["c1"], bST["c2"]], writes=[bdsl])
            dma("sp", ln2_dram[t * 128:(t + 1) * 128, :], dslot[:, :],
                [bLn2D], [bdsl], SLN[sl])
            dma("sp", del_dram[t * 128:(t + 1) * 128, :], bufC[:, 0:1024],
                [bDelD], [bGb[0]], SDEL)

        w1q = [(e_, fs_) for e_ in range(E) for fs_ in range(16)]
        w2q = [(e_, dc_) for e_ in range(E) for dc_ in range(8)]
        wp = {"w1": 0, "w2": 0}
        def pf1():
            if wp["w1"] < len(w1q):
                e_, fs_ = w1q[wp["w1"]]
                wp["w1"] += 1
                sl_ = fs_ % 2
                dma("pool", wbuf[:, sl_ * 2048:(sl_ + 1) * 2048],
                    w1_d[e_, fs_], [bw1s[sl_]], [], SW1[sl_])
        def pf2():
            if wp["w2"] < len(w2q):
                e_, dc_ = w2q[wp["w2"]]
                wp["w2"] += 1
                sl_ = dc_ % 2
                dma("pool", wbuf[:, 4096 + sl_ * 4096: 4096 + (sl_ + 1) * 4096],
                    w2_d[e_, dc_], [bw2s[sl_]], [], SW2[sl_])
        pf1(); pf1(); pf2(); pf2()

        bGh = [bGhT, bHt2]
        dtT = bufC[:, 12288:12288 + 8 * CAP].rearrange("p (c q) -> p c q", c=8)

        defq = []
        def gather_emit(e):
            par = e % 2
            for g in range(NG):
                gb = bufC[:, g * 1024:(g + 1) * 1024]
                col = e * NG + g
                P.emit("pool", lambda EG, gb=gb, col=col: EG.indirect_dma_start(
                    out=gb, out_offset=None,
                    in_=ln2_dram,
                    in_offset=IndirectOffsetOnAxis(ap=posm_i[:, col:col + 1], axis=0)),
                    reads=[bPosmI, bLn2D], writes=[bGb[g]],
                    sem=P.sems["sg%d" % g])
        def defer_gather_tr(e):
            par = e % 2
            gbase = 4096 + par * 4096
            for g in range(NG):
                def emit_tr(g=g, gbase=gbase, par=par):
                    gb = bufC[:, g * 1024:(g + 1) * 1024]
                    half = g % 2
                    pshalf = psT[:, half * 1024:half * 1024 + 1024]
                    def _trg(EG, gb=gb, ph=pshalf):
                        last = None
                        for c in range(8):
                            last = EG.transpose(ph[:, c * 128:(c + 1) * 128],
                                                gb[:, c * 128:(c + 1) * 128], idb[:, :])
                        return last
                    P.emit("pe", _trg, reads=[bGb[g], bConst], writes=[bPsT[half]])
                    src_t = pshalf.rearrange("p (c q) -> p c q", c=8)
                    dst_t = bufC[:, gbase:gbase + 8 * CAP].rearrange(
                        "p (c q) -> p c q", c=8)[:, :, g * 128:(g + 1) * 128]
                    P.emit("dve", lambda EG, d2=dst_t, s2=src_t: EG.tensor_copy(d2, s2),
                           reads=[bPsT[half]], writes=[bGh[par]])
                defq.append(emit_tr)
            def emit_ht(e=e, gbase=gbase, par=par):
                def _ht(EG, e=e, gbase=gbase):
                    last = None
                    for c in range(8):
                        sl_ = bufC[:, gbase + c * CAP: gbase + (c + 1) * CAP]
                        last = EG.tensor_scalar(
                            sl_, sl_,
                            gamsb[:, e * 8 + c: e * 8 + c + 1],
                            betsb[:, e * 8 + c: e * 8 + c + 1],
                            ALU.mult, ALU.add)
                    return last
                P.emit("dve", _ht, reads=[bGh[par], bConst], writes=[bGh[par]])
            defq.append(emit_ht)
        def defer_back_tr(e):
            par = e % 2
            for g in range(NG):
                def emit_bt(g=g, e=e):
                    half = g % 2
                    pshalf = psT[:, half * 1024:half * 1024 + 1024]
                    def _trd(EG, ph=pshalf, g=g):
                        last = None
                        for c in range(8):
                            last = EG.transpose(ph[:, c * 128:(c + 1) * 128],
                                                dtT[:, c, g * 128:(g + 1) * 128], idb[:, :])
                        return last
                    P.emit("pe", _trd, reads=[bDtT, bConst], writes=[bPsT[half]])
                    dslot = dtok if g % 2 == 0 else dtok2
                    bdsl = bDtok if g % 2 == 0 else bDt2
                    P.emit("dve", lambda EG, ph=pshalf, o=dslot: EG.tensor_copy(o[:, :], ph),
                           reads=[bPsT[half]], writes=[bdsl])
                    col = e * NG + g
                    P.emit("pool", lambda EG, o=dslot, col=col: EG.indirect_dma_start(
                        out=del_dram, out_offset=IndirectOffsetOnAxis(
                            ap=posm_i[:, col:col + 1], axis=0),
                        in_=o[:, :], in_offset=None),
                        reads=[bPosmI, bdsl, bDelD], writes=[bDelD], sem=SDS)
                defq.append(emit_bt)

        if b == 0:
            _p0 = phase0_closures(1)
            pending_p0.extend(_p0[:-1])
            pending_p0_post.append(_p0[-1])
        gather_emit(0)
        defer_gather_tr(0)
        while defq:
            defq.pop(0)()
        for e in range(E):
            par = e % 2
            gbase = 4096 + par * 4096
            if e + 1 < E:
                gather_emit(e + 1)
                defer_gather_tr(e + 1)
            for fc in range(32):
                fs, fc2 = fc // 2, fc % 2
                wsl = fs % 2
                pst, off, bbank = mm1rot[fc % 4]
                out_ap = pst[:, off:off + CAP]
                pairs = [(wbuf[:, wsl * 2048 + dcc * 256 + fc2 * 128:
                               wsl * 2048 + dcc * 256 + fc2 * 128 + 128],
                          bufC[:, gbase + dcc * CAP: gbase + (dcc + 1) * CAP])
                         for dcc in range(8)]
                mm_group(out_ap, pairs, reads=[bw1s[wsl], bGh[par]], writes=[bbank])
                P.emit("act", lambda EG, o=bufD[:, fc * CAP:(fc + 1) * CAP], i=out_ap,
                       bc=b1sb[:, e * 32 + fc: e * 32 + fc + 1]:
                       EG.activation(o, i, AF.Relu, bias=bc),
                       reads=[bbank, bConst], writes=[bD])
                if fc % 2 == 1:
                    pf1()
                if fc % 3 == 2 and defq:
                    defq.pop(0)()
                if fc % 4 == 1 and pending_p0:
                    pending_p0.pop(0)()

            for dc in range(8):
                wsl = dc % 2
                pst, pb = (psE, bPsE) if dc % 2 == 0 else (psF, bPsF)
                pairs = [(wbuf[:, 4096 + wsl * 4096 + fc * 128:
                               4096 + wsl * 4096 + (fc + 1) * 128],
                          bufD[:, fc * CAP:(fc + 1) * CAP])
                         for fc in range(32)]
                mm_group(pst[:, 0:CAP], pairs, reads=[bw2s[wsl], bD], writes=[pb])
                P.emit("act", lambda EG, o=bufC[:, 12288 + dc * CAP: 12288 + (dc + 1) * CAP],
                       i=pst[:, 0:CAP], bc=b2sb[:, e * 8 + dc: e * 8 + dc + 1]:
                       EG.activation(o, i, AF.Identity, bias=bc),
                       reads=[pb, bConst], writes=[bDtT])
                pf2()

            defer_back_tr(e)
        while defq:
            defq.pop(0)()

        def final_ops(b=b):
            tiles = []
            for t in range(NT):
                def mk(t=t, b=b):
                    sl = t % 2
                    xfs = xf[:, sl * 1024:sl * 1024 + 1024]
                    bxf = bXF[sl]
                    dslot = dtok if sl == 0 else dtok2
                    bdsl = bDtok if sl == 0 else bDt2
                    tfX = tfA if t % 2 == 0 else tfA2
                    bTfX = bTfA if t % 2 == 0 else bTfA2
                    ssl_ = t % 4
                    sap = bufE[:, ssl_ * 1024: ssl_ * 1024 + 1024]
                    steps = []
                    steps.append(lambda: dma("sp", xfs,
                                             x_d[b, t * 128:(t + 1) * 128, :],
                                             [bxf], [], SX[sl]))
                    steps.append(lambda: dma("sp", dslot[:, :],
                                             del_dram[t * 128:(t + 1) * 128, :],
                                             [bdsl], [bDelD], SDL[sl]))
                    steps.append(lambda: dma("sp", sap,
                                             s_dram[t * 128:(t + 1) * 128, :],
                                             [bPT[ssl_]], [bSd], SSD[ssl_]))
                    def _fc1(EG, i=xfs, t=t, ds=dslot, o=tfX, bc0=b * NT):
                        EG.tensor_scalar(o[:, :], i,
                                         st["mu1"][:, bc0 + t:bc0 + t + 1],
                                         st["rstd1"][:, bc0 + t:bc0 + t + 1],
                                         ALU.subtract, ALU.mult)
                        return EG.scalar_tensor_tensor(
                            o[:, :], ds[:, :],
                            st["alpha"][:, bc0 + t:bc0 + t + 1],
                            o[:, :], ALU.mult, ALU.add)
                    steps.append(lambda: P.emit(
                        "dve", _fc1,
                        reads=[bxf, bdsl, bST["mu1"], bST["rstd1"], bST["alpha"]],
                        writes=[bTfX]))
                    steps.append(lambda: P.emit(
                        "act", lambda EG, ds=dslot, i=tfX:
                        EG.activation(ds[:, :], i[:, :], AF.Silu),
                        reads=[bTfX], writes=[bdsl]))
                    def _fc2(EG, i=xfs, ds=dslot, o=tfX, sp2=sap):
                        EG.tensor_tensor(o[:, :], ds[:, :], sp2, ALU.mult)
                        return EG.tensor_tensor(o[:, :], i, o[:, :], ALU.add)
                    steps.append(lambda: P.emit(
                        "dve", _fc2, reads=[bxf, bdsl, bPT[ssl_]],
                        writes=[bTfX, bOutT]))
                    steps.append(lambda: dma("sp", out_d[b, t * 128:(t + 1) * 128, :],
                                             tfX[:, :], [], [bOutT, bTfX], SO))
                    return steps
                tiles.append(mk())
            ops = list(tiles[0][:3])
            for t in range(NT):
                if t + 1 < NT:
                    ops.extend(tiles[t + 1][:3])
                ops.extend(tiles[t][3:])
            return ops
        if b == 0:
            pending_final.extend(final_ops())
        else:
            for op_ in final_ops():
                op_()

        for q4_ in range(4):
            bA4[q4_].merge_from(bA)
        bC.merge_from(bGb[0], bGb[1], bGb[2], bGb[3], bGhT, bHt2, bDtT)
        bW0.merge_from(bw1s[0], bw1s[1], bw2s[0])
        bW1.merge_from(bw2s[1])
        bE.merge_from(*bPT)

    P.plan["pool"].append(("w", SO, SO.v))
    P.check()

    with ExitStack() as es:
        for nm in P.sems:
            P.sems[nm].h = es.enter_context(nc.semaphore(nm))
        block = es.enter_context(nc.Block())
        block.tensor(lambda EG: P.run("pe", EG))
        block.scalar(lambda EG: P.run("act", EG))
        block.vector(lambda EG: P.run("dve", EG))
        block.gpsimd(lambda EG: P.run("pool", EG))
        block.sync(lambda EG: P.run("sp", EG))
    return nc


def kernel(L0_attn_x, L0_attn_Wq, L0_attn_Wk, L0_attn_Wv, L0_attn_Wo,
           L0_ffn_centroids, L0_ffn_gamma, L0_ffn_beta, L0_ffn_W1, L0_ffn_b1,
           L0_ffn_W2, L0_ffn_b2):
    x = np.ascontiguousarray(np.asarray(L0_attn_x, dtype=np.float32))
    Wq = np.ascontiguousarray(np.asarray(L0_attn_Wq, dtype=np.float32))
    Wk = np.ascontiguousarray(np.asarray(L0_attn_Wk, dtype=np.float32))
    Wv = np.ascontiguousarray(np.asarray(L0_attn_Wv, dtype=np.float32))
    Wo = np.ascontiguousarray(np.asarray(L0_attn_Wo, dtype=np.float32))
    cen = np.asarray(L0_ffn_centroids, dtype=np.float32)
    gam = np.asarray(L0_ffn_gamma, dtype=np.float32)
    bet = np.asarray(L0_ffn_beta, dtype=np.float32)
    W1 = np.asarray(L0_ffn_W1, dtype=np.float32)
    b1 = np.asarray(L0_ffn_b1, dtype=np.float32)
    W2 = np.asarray(L0_ffn_W2, dtype=np.float32)
    b2 = np.asarray(L0_ffn_b2, dtype=np.float32)

    w1h = np.ascontiguousarray(
        W1.reshape(E, 16, 256, 8, 128).transpose(0, 1, 4, 3, 2)).reshape(E, 16, 128, 2048)
    w2h = np.ascontiguousarray(
        W2.reshape(E, 8, 128, 32, 128).transpose(0, 1, 4, 3, 2)).reshape(E, 8, 128, 4096)
    b1h = np.ascontiguousarray(b1.reshape(E, 32, 128).transpose(2, 0, 1)).reshape(128, E * 32)
    b2h = np.ascontiguousarray(b2.reshape(E, 8, 128).transpose(2, 0, 1)).reshape(128, E * 8)
    gamh = np.ascontiguousarray(gam.reshape(E, 8, 128).transpose(2, 0, 1)).reshape(128, E * 8)
    beth = np.ascontiguousarray(bet.reshape(E, 8, 128).transpose(2, 0, 1)).reshape(128, E * 8)
    cenh = np.ascontiguousarray(cen.T.reshape(8, 128, 8).transpose(1, 0, 2)).reshape(128, 64)

    idfc = np.eye(128, dtype=np.float32)
    idbc = np.eye(128, dtype=np.float32).astype(ml_dtypes.bfloat16)
    mkbc = (np.arange(896)[None, :] >= (np.arange(128)[:, None] + 384)).astype(
        ml_dtypes.bfloat16)
    onec = np.ones((1, 64), dtype=np.float32)
    ltc = (np.arange(128)[:, None] < np.arange(128)[None, :]).astype(np.float32)
    o1rc = np.ones((1, 128), dtype=np.float32)
    o1cc = np.ones((128, 1), dtype=np.float32)
    cvecc = (np.arange(8, dtype=np.float32) * CAP).reshape(1, 8)
    p2tic = np.full((NSLOT, 128), float(T), dtype=np.float32)
    tokfc = (np.arange(NT, dtype=np.float32)[None, :] * 128
             + np.arange(128, dtype=np.float32)[:, None])

    nc = build_nc()
    shared = dict(wq=Wq, wk=Wk, wv=Wv, wo=Wo, w1=w1h, w2=w2h, b1=b1h, b2=b2h,
                  gam=gamh, bet=beth, cen=cenh, idf=idfc, idb=idbc, mkb=mkbc,
                  ones=onec, lt=ltc, o1r=o1rc, o1c=o1cc, cvec=cvecc,
                  p2ti=p2tic, tokf=tokfc)
    in_maps = [dict(shared, x=np.ascontiguousarray(x[2 * c:2 * c + 2]))
               for c in range(8)]
    res = run_bass_kernel_spmd(nc, in_maps, core_ids=list(range(8)))
    return np.concatenate([r["out"] for r in res.results], axis=0)
